# version 02
# baseline (speedup 1.0000x reference)
import numpy as np
import ml_dtypes
import concourse.bass as bass
import concourse.mybir as mybir
from concourse.bass_utils import run_bass_kernel_spmd

F32 = mybir.dt.float32
BF16 = mybir.dt.bfloat16
FP16 = mybir.dt.float16
I8 = mybir.dt.int8
AF = mybir.ActivationFunctionType
ALU = mybir.AluOpType
AX = mybir.AxisListType
BF = ml_dtypes.bfloat16

T = 2048
D = 1024
NCH = 8
SP = 512
NSP = T // SP
C = 64
NLEV = 5
CRND = 12582912.0
LN8 = float(np.log(0.125))

ENGS = ('tensor', 'vector', 'scalar', 'sync', 'gpsimd')
CONST_VALUES = [1.0, -1.0, 0.5, -0.5, 1e-30, 1e-5, 1e-6, LN8, -1e-5, 1e-8, -1e-8]


def call(method, *args, **kw):
    def fn(e):
        return getattr(e, method)(*args, **kw)
    fn.method = method
    fn.args = args
    fn.kw = kw
    import sys
    f = sys._getframe(1)
    fn.site = "%s:%d" % (f.f_code.co_filename.rsplit('/', 1)[-1], f.f_lineno)
    return fn


def _is_ap(o):
    return hasattr(o, 'tensor') and hasattr(o, 'ap') and hasattr(o, 'offset')


def _ap_elems(a):
    idx = np.array([int(a.offset)], dtype=np.int64)
    for st_, n in a.ap:
        st_, n = int(st_), int(n)
        idx = (idx[:, None] + (np.arange(n, dtype=np.int64) * st_)[None, :]).reshape(-1)
    return a.tensor, idx


class APCheck:
    MAXEL = 3 << 20

    def __init__(self):
        self.kl = list(ENGS) + ['dma:' + e for e in ENGS]
        self.ki = {k: i for i, k in enumerate(self.kl)}
        self.nk = len(self.kl)
        self.vecs = {k: [] for k in self.kl}
        self.lw = {}
        self.rd = {}
        self.meta = {}
        self.opinfo = [None]
        self.skipped = set()
        self.pre = set()
        self.viol = {}

    def _tarr(self, t):
        tid = id(t)
        if tid in self.lw:
            return tid
        if tid in self.skipped:
            return None
        n = 1
        for d in t.shape:
            n *= int(d)
        if n > self.MAXEL:
            self.skipped.add(tid)
            return None
        self.lw[tid] = np.zeros(n, np.int32)
        self.rd[tid] = {}
        self.meta[tid] = (t.name, n)
        return tid

    def mark_pre(self, t):
        self.pre.add(id(t))

    def _report(self, kind, site, m, t, wsite, nel):
        key = (kind, site, m, t.name, wsite)
        c = self.viol.get(key, 0)
        self.viol[key] = c + nel
        if c == 0:
            print("!! AP-%s: %s %s on %s (writer/reader at %s)"
                  % (kind, site, m, t.name, wsite))

    def op(self, eng, idx, dma, fn, waits):
        ei = self.ki[eng]
        prev = self.vecs[eng][idx - 1] if idx > 0 else np.zeros(self.nk, np.int32)
        v = prev.copy()
        for (k, i) in waits:
            base = k[4:] if k.startswith('dma:') else k
            wv = self.vecs[base][i]
            np.maximum(v, wv, out=v)
            if v[self.ki[k]] < i:
                v[self.ki[k]] = i
        v[ei] = idx
        self.vecs[eng].append(v)
        ckey = ('dma:' + eng) if dma else eng
        cki = self.ki[ckey]
        m = getattr(fn, 'method', None)
        if m is None:
            return
        args, kw = fn.args, fn.kw
        wr, rd = [], []
        aps = [a for a in args if _is_ap(a)]
        if m == 'matmul' and not kw.get('start', False) and aps:
            rd.append(aps[0])
        if aps:
            wr.append(aps[0])
            rd.extend(aps[1:])
        for kname, kv in kw.items():
            if _is_ap(kv):
                (wr if kname == 'accum_out' else rd).append(kv)
        opnum = len(self.opinfo)
        self.opinfo.append((ckey, idx, getattr(fn, 'site', '?'), m))
        for a in rd:
            t, ix = _ap_elems(a)
            tid = self._tarr(t)
            if tid is None or tid in self.pre:
                continue
            lw = self.lw[tid]
            ws = np.unique(lw[ix])
            for w in ws:
                if w == 0:
                    self._report('UBD', self.opinfo[opnum][2], m, t, '-',
                                 int((lw[ix] == 0).sum()))
                    continue
                wk, wi, wsite, wm = self.opinfo[w]
                if wk != ckey and v[self.ki[wk]] < wi:
                    self._report('RAW', self.opinfo[opnum][2], m, t, wsite,
                                 int((lw[ix] == w).sum()))
            ra = self.rd[tid].setdefault(cki, np.zeros(self.meta[tid][1], np.int32))
            ra[ix] = opnum
        for a in wr:
            t, ix = _ap_elems(a)
            tid = self._tarr(t)
            if tid is None:
                continue
            lw = self.lw[tid]
            ws = np.unique(lw[ix])
            for w in ws:
                if w == 0:
                    continue
                wk, wi, wsite, wm = self.opinfo[w]
                if wk != ckey and v[self.ki[wk]] < wi:
                    self._report('WAW', self.opinfo[opnum][2], m, t, wsite,
                                 int((lw[ix] == w).sum()))
            for kk, ra in self.rd[tid].items():
                if kk == cki:
                    continue
                rs = np.unique(ra[ix])
                for r in rs:
                    if r == 0:
                        continue
                    rk, rdi, rsite, rm = self.opinfo[r]
                    if v[self.ki[rk]] < rdi:
                        self._report('WAR', self.opinfo[opnum][2], m, t, rsite,
                                     int((ra[ix] == r).sum()))
            lw[ix] = opnum
        return

    def summary(self):
        print("== APCheck: %d distinct violation sites" % len(self.viol))
        for k, n in sorted(self.viol.items(), key=lambda x: -x[1])[:60]:
            print("  %s x%d: op@%s %s on %s peer@%s" % (k[0], n, k[1], k[2], k[3], k[4]))


class Prog:
    def __init__(self, nc):
        self.nc = nc
        self.paranoid = False
        self.ops = {e: [] for e in ENGS}
        self.marked = {e: set() for e in ENGS}
        self.tiles = {}
        self.barrier_deps = []
        self.sems = {}
        self.gseq = 0
        self.live = {}
        self.freed = {}
        self.viol = 0
        self.apc = APCheck() if APCHECK else None
        self.haz_recent = {'vector': [], 'scalar': []}
        self.haz_junk = {}
        self.nspacers = 0

    HAZ_WIN = 2
    HAZ_SIZE = 256

    @staticmethod
    def _ap_bounds(a):
        lo = int(a.offset)
        hi = lo
        nfree = 1
        dims = list(a.ap)
        for di, (st_, num) in enumerate(dims):
            st_, num = int(st_), int(num)
            if di > 0 or len(dims) == 1:
                nfree *= num
            if st_ >= 0:
                hi += (num - 1) * st_
            else:
                lo += (num - 1) * st_
        return id(a.tensor), lo, hi + 1, nfree

    def _haz_guard(self, eng, fn):
        if eng not in self.haz_recent or eng not in self.haz_junk:
            return
        m = getattr(fn, 'method', None)
        if m is None:
            return
        aps = [a for a in fn.args if _is_ap(a)] + \
              [v for v in fn.kw.values() if _is_ap(v)]
        if not aps:
            return
        reads = aps[1:] if len(aps) > 1 else []
        if m == 'matmul' and not fn.kw.get('start', False):
            reads = aps
        rb = [self._ap_bounds(a) for a in reads]
        need = 0
        for dist, wlist in enumerate(reversed(self.haz_recent[eng])):
            for (tid, lo, hi, n) in wlist:
                for (rt, rl, rh, rn) in rb:
                    if rt == tid and rl < hi and lo < rh:
                        need = max(need, self.HAZ_WIN - dist)
        for _ in range(need):
            self.nspacers += 1
            jk = self.haz_junk[eng]
            if eng == 'vector':
                f = call('tensor_copy', jk[:, 0:512], jk[:, 512:1024])
            else:
                f = call('copy', jk[:, 0:512], jk[:, 512:1024])
            self._emit_plain(eng, f)

    def _emit_plain(self, eng, fn):
        lst = self.ops[eng]
        idx = len(lst)
        if self.apc is not None:
            self.apc.op(eng, idx, False, fn, [])
        lst.append({'waits': [], 'fn': fn, 'dma': False, 'idx': idx})
        self.haz_recent[eng].append([])
        del self.haz_recent[eng][:-self.HAZ_WIN]

    def _haz_record(self, eng, fn):
        if eng not in self.haz_recent:
            return
        ws = []
        m = getattr(fn, 'method', None)
        if m is not None:
            aps = [a for a in fn.args if _is_ap(a)]
            wr = aps[:1] + [v for k, v in fn.kw.items()
                            if k == 'accum_out' and _is_ap(v)]
            for a in wr:
                b = self._ap_bounds(a)
                if b[3] <= self.HAZ_SIZE:
                    ws.append(b)
        self.haz_recent[eng].append(ws)
        del self.haz_recent[eng][:-self.HAZ_WIN]

    def _alloc(self, h):
        import concourse.mybir as _mb
        per = 1
        for d in h.shape[1:]:
            per *= int(d)
        per *= _mb.dt.size(h.dtype)
        addr = self.nc.lookup_mloc(h).addr
        rec = {'name': h.name, 'addr': addr, 'size': per, 'h': h,
               'a_seq': self.gseq}
        for r2 in self.live.values():
            if addr < r2['addr'] + r2['size'] and r2['addr'] < addr + per:
                self.viol += 1
                print("!! ALLOC-OVERLAP:", h.name, "(%d+%d)" % (addr, per),
                      "vs", r2['name'], "(%d+%d)" % (r2['addr'], r2['size']))
        self.live[id(h)] = rec

    def _free(self, h):
        rec = self.live.pop(id(h), None)
        if rec is not None:
            rec['f_seq'] = self.gseq
            self.freed[id(h)] = rec

    def _check_keys(self, eng, keys, mode):
        for t in keys:
            o = t[0] if isinstance(t, tuple) else t
            if id(o) in self.freed:
                self.viol += 1
                print("!! ZOMBIE-%s: op %s seq %d touches freed tensor %s"
                      % (mode, eng, self.gseq, self.freed[id(o)]['name']))

    @staticmethod
    def _tkey(t):
        if isinstance(t, tuple):
            return (id(t[0]), t[1])
        return (id(t), None)

    def _tile(self, t):
        k = self._tkey(t)
        if k not in self.tiles:
            self.tiles[k] = {'w': None, 'r': {}}
        return self.tiles[k]

    def op(self, eng, fn, reads=(), writes=(), dma=False):
        if not dma:
            self._haz_guard(eng, fn)
        if self.paranoid:
            self.barrier()
        self.gseq += 1
        self._check_keys(eng, reads, 'R')
        self._check_keys(eng, writes, 'W')
        lst = self.ops[eng]
        idx = len(lst)
        me = ('dma:' + eng) if dma else eng
        waits = []

        def need(dep):
            if dep is None:
                return
            k, i = dep
            if k == eng:
                return
            waits.append((k, i))

        for t in reads:
            rec = self._tile(t)
            if rec['w'] is None:
                o = t[0] if isinstance(t, tuple) else t
                nm = getattr(o, 'name', type(o).__name__)
                tg = t[1] if isinstance(t, tuple) else None
                self.viol += 1
                print("!! USE-BEFORE-DEF: %s reads (%s, %r) at seq %d"
                      % (eng, nm, tg, self.gseq))
            need(rec['w'])
            rec['r'][me] = idx
        for t in writes:
            rec = self._tile(t)
            need(rec['w'])
            for rk, ri in list(rec['r'].items()):
                if rk == eng:
                    continue
                waits.append((rk, ri))
            rec['w'] = (me, idx)
            rec['r'] = {}
        waits.extend(self.barrier_deps)
        if self.apc is not None:
            self.apc.op(eng, idx, dma, fn, waits)
        lst.append({'waits': waits, 'fn': fn, 'dma': dma, 'idx': idx})
        if dma:
            if eng in self.haz_recent:
                self.haz_recent[eng].append([])
                del self.haz_recent[eng][:-self.HAZ_WIN]
        else:
            self._haz_record(eng, fn)
        for (k, i) in waits:
            if not k.startswith('dma:'):
                self.marked[k].add(i)
        return (me, idx)

    def barrier(self):
        deps = []
        for e in ENGS:
            if not self.ops[e]:
                continue
            last = len(self.ops[e]) - 1
            ent = self.ops[e][last]
            if ent['dma']:
                deps.append(('dma:' + e, last))
            else:
                deps.append((e, last))
                self.marked[e].add(last)
            for j in range(last, -1, -1):
                if self.ops[e][j]['dma']:
                    if not ent['dma']:
                        deps.append(('dma:' + e, j))
                    break
        self.barrier_deps = deps

    def finalize(self):
        nc = self.nc
        rank = {}
        for e in ENGS:
            ms = sorted(self.marked[e])
            rank[e] = {i: r + 1 for r, i in enumerate(ms)}
        dmarank = {e: {} for e in ENGS}
        for e in ENGS:
            cnt = 0
            for j, ent in enumerate(self.ops[e]):
                if ent['dma']:
                    cnt += 1
                    dmarank[e][j] = cnt * 16

        from contextlib import ExitStack
        with ExitStack() as st:
            for e in ENGS:
                self.sems[e] = st.enter_context(nc.semaphore("sem_" + e))
                self.sems['dma:' + e] = st.enter_context(nc.semaphore("dsem_" + e))
            block = st.enter_context(nc.Block())

            def make_body(ename):
                ops = self.ops[ename]
                mset = self.marked[ename]
                dsem = self.sems['dma:' + ename]
                esem = self.sems[ename]
                sems = self.sems
                rk = rank
                dr = dmarank

                def body(eng):
                    lastw = {}
                    for ent in ops:
                        for (k, i) in ent['waits']:
                            if k.startswith('dma:'):
                                v = dr[k[4:]][i]
                            else:
                                v = rk[k][i]
                            if lastw.get(k, 0) >= v:
                                continue
                            eng.wait_ge(sems[k], v)
                            lastw[k] = v
                        if ent['dma'] and SERIALIZE_DMA:
                            v = dr[ename][ent['idx']] - 16
                            k = 'dma:' + ename
                            if v > 0 and lastw.get(k, 0) < v:
                                eng.wait_ge(dsem, v)
                                lastw[k] = v
                        inst = ent['fn'](eng)
                        if ent['dma']:
                            inst.then_inc(dsem, 16)
                        elif ent['idx'] in mset:
                            inst.then_inc(esem, 1)
                return body

            for ename in ENGS:
                getattr(block, ename)(make_body(ename))


def _tile_lhsT(WT, n_ic, n_oc):
    return np.ascontiguousarray(
        WT.reshape(n_ic, 128, n_oc, 128).transpose(2, 1, 0, 3))


def _conv_diag(conv):
    out = np.zeros((8, 128, 4, 128), dtype=np.float32)
    for oc in range(8):
        for p in range(128):
            out[oc, p, :, p] = conv[oc * 128 + p, :]
    return out


def _host_prep(inputs):
    H = {}
    f = np.ascontiguousarray
    H['wq'] = _tile_lhsT(f(inputs['L0_attn_Wq'].T), 8, 8)
    H['wk'] = _tile_lhsT(f(inputs['L0_attn_Wk'].T), 8, 8)
    H['wv'] = _tile_lhsT(f(inputs['L0_attn_Wv'].T), 8, 8)
    H['wo'] = _tile_lhsT(f(inputs['L0_attn_Wo'].T), 8, 8)
    H['wb'] = f(inputs['L0_attn_Wb'].T.reshape(8, 128, 16).transpose(1, 0, 2))
    H['cdq'] = _conv_diag(inputs['L0_attn_conv_q'])
    H['cdk'] = _conv_diag(inputs['L0_attn_conv_k'])
    H['cdv'] = _conv_diag(inputs['L0_attn_conv_v'])
    H['onw'] = f(np.broadcast_to(inputs['L0_attn_o_norm_w'], (128, 64))).astype(np.float32)
    H['wgate'] = _tile_lhsT(f(inputs['L0_ffn_w_gate'].T), 8, 32)
    H['wup'] = _tile_lhsT(f(inputs['L0_ffn_w_up'].T), 8, 32)
    H['wdown'] = _tile_lhsT(f(inputs['L0_ffn_w_down'].T), 32, 8)
    wqkv = inputs['L1_attn_wqkv']
    perm = []
    for part in range(3):
        for h in range(16):
            base = h * 192 + part * 64
            perm.extend(range(base, base + 64))
    H['wqkv'] = _tile_lhsT(f(wqkv[:, perm]), 8, 24)
    H['w1'] = _tile_lhsT(f(inputs['L1_ffn_W1'].T), 8, 32)
    H['w2'] = _tile_lhsT(f(inputs['L1_ffn_W2'].T), 32, 8)
    H['b1'] = f(inputs['L1_ffn_b1'].reshape(32, 128).T)
    H['b2'] = f(inputs['L1_ffn_b2'].reshape(8, 128).T)
    sl = np.tril(np.ones((C, C), dtype=np.float32), -1)
    z = np.zeros((C, C), dtype=np.float32)
    msl = np.block([[sl, z], [z, sl]])
    mm = np.concatenate([msl, msl.T], axis=1)
    H['mmask2'] = f(np.concatenate([mm, mm], axis=1)).astype(BF)
    tiu = np.triu(np.ones((C, C), dtype=np.float32), 0)
    tb = np.block([[tiu, z], [z, tiu]])
    H['tiumask4'] = f(np.concatenate([tb] * 4, axis=1)).astype(BF)
    H['ident'] = np.eye(128, dtype=BF)
    H['identf'] = np.eye(128, dtype=np.float32)
    bd = np.zeros((128, 128), dtype=np.float32)
    bd[:64, :64] = 1.0
    bd[64:, 64:] = 1.0
    H['ind2'] = bd.astype(BF)
    H['ind2e'] = (bd * 0.125).astype(BF)
    b2w = np.zeros((2, 128), dtype=np.float32)
    b2w[0, :64] = 1.0
    b2w[1, 64:] = 1.0
    H['bc2'] = b2w.astype(BF)
    H['onescol'] = np.ones((128, 1), dtype=BF)
    H['onescol_f'] = np.ones((128, 2), dtype=np.float32)
    H['half2'] = np.full((2, 128), 0.5, dtype=np.float32)
    H['onesrow_f'] = np.ones((1, 128), dtype=np.float32)
    return H


MASK_SPECS = [("mmask2", (128, 512), BF16), ("tiumask4", (128, 512), BF16)]
CONST_SPECS = [("ident", (128, 128), BF16), ("identf", (128, 128), F32),
               ("ind2", (128, 128), BF16),
               ("ind2e", (128, 128), BF16), ("bc2", (2, 128), BF16),
               ("onescol", (128, 1), BF16), ("onw", (128, 64), F32),
               ("onescol_f", (128, 2), F32), ("onesrow_f", (1, 128), F32),
               ("half2", (2, 128), F32)]

WEIGHT_SPECS = [("wq", (8, 128, 8, 128)), ("wk", (8, 128, 8, 128)),
                ("wv", (8, 128, 8, 128)), ("wo", (8, 128, 8, 128)),
                ("wb", (128, 8, 16)),
                ("cdq", (8, 128, 4, 128)), ("cdk", (8, 128, 4, 128)),
                ("cdv", (8, 128, 4, 128)),
                ("wgate", (32, 128, 8, 128)), ("wup", (32, 128, 8, 128)),
                ("wdown", (8, 128, 32, 128)),
                ("wqkv", (24, 128, 8, 128)),
                ("w1", (32, 128, 8, 128)), ("w2", (8, 128, 32, 128)),
                ("b1", (128, 32)), ("b2", (128, 8))]

QSPEC_L0 = [("wgate", "wgq", 32, 8, 0, 1e-5), ("wup", "wuq", 32, 8, 1, 1e-5),
            ("wdown", "wdq", 8, 32, 2, 1e-5)]
QSPEC_L1 = [("w1", "w1q", 32, 8, 3, 1e-8), ("w2", "w2q", 8, 32, 4, 1e-8)]


PHASES = 99
RELAX = None
A2_PARANOID = False
L8_STATS_PAR = False
L8_COPY_PAR = False
L8_LN_PAR = False
DUMP8 = "lni"
DBG_SCR = False
DUMP4 = False
SERIALIZE_DMA = True
APCHECK = False
GLOBAL_PARANOID = False
P4_PRE_PAR = False
P4_LN_PAR = False
LN_A_RELAX = False
LN_B_RELAX = False
LN_C_RELAX = False
LN_D_RELAX = False

class _Track:
    def __init__(self, P, cm):
        self.P = P
        self.cm = cm

    def __enter__(self):
        h = self.cm.__enter__()
        self.h = h
        self.P._alloc(h)
        return h

    def __exit__(self, *e):
        self.P._free(self.h)
        return self.cm.__exit__(*e)


def build_program():
    from contextlib import ExitStack
    nc = bass.Bass("TRN2", target_bir_lowering=False, debug=False)
    P = Prog(nc)

    def SBT(*args):
        return _Track(P, nc.sbuf_tensor(*args))

    din = {}
    din["x_in"] = nc.dram_tensor("x_in", [8, 128, T], F32, kind="ExternalInput").ap()
    for nm, shp in WEIGHT_SPECS:
        din[nm] = nc.dram_tensor(nm, list(shp), F32, kind="ExternalInput").ap()
    for nm, shp, dt in CONST_SPECS + MASK_SPECS:
        din[nm] = nc.dram_tensor(nm, list(shp), dt, kind="ExternalInput").ap()
    x_out = nc.dram_tensor("x_out", [8, 128, T], F32, kind="ExternalOutput").ap()
    s_out = nc.dram_tensor("s_out", [8, 128, T], F32,
                           kind="ExternalOutput").ap() if DBG_SCR else None
    dbg_raw = nc.dram_tensor("dbg_raw", [128, T + 8], BF16,
                             kind="ExternalOutput").ap() if DBG_SCR else None
    dbg_sil = nc.dram_tensor("dbg_sil", [128, T], BF16,
                             kind="ExternalOutput").ap() if DBG_SCR else None
    dbg_ps = nc.dram_tensor("dbg_ps", [128, 1024], F32,
                            kind="ExternalOutput").ap() if DBG_SCR else None
    if P.apc is not None:
        for _ap in din.values():
            P.apc.mark_pre(_ap.tensor)
    if GLOBAL_PARANOID:
        P.paranoid = True

    scr = {}
    for nm, shape, dt in [("qh", (8, 128, T), BF16), ("kh", (8, 128, T), BF16),
                          ("kbn", (8, 128, T), BF16), ("vb", (8, 128, T), BF16),
                          ("xn_d", (8, 128, T), BF16),
                          ("lnflat", (4096,), F32), ("bflat", (16, T), BF16),
                          ("nbflat", (16, T), BF16),
                          ("wgq", (32, 128, 8, 128), BF16), ("wuq", (32, 128, 8, 128), BF16),
                          ("wdq", (8, 128, 32, 128), BF16),
                          ("w1q", (32, 128, 8, 128), BF16), ("w2q", (8, 128, 32, 128), BF16),
                          ("wqB", (8, 128, 8, 128), BF16), ("wkB", (8, 128, 8, 128), BF16),
                          ("wvB", (8, 128, 8, 128), BF16), ("woB", (8, 128, 8, 128), BF16),
                          ("wbB", (128, 8, 16), BF16), ("cdqB", (8, 128, 4, 128), BF16),
                          ("cdkB", (8, 128, 4, 128), BF16), ("cdvB", (8, 128, 4, 128), BF16),
                          ("qkvB", (24, 128, 8, 128), BF16)]:
        scr[nm] = nc.dram_tensor("scr_" + nm, list(shape), dt,
                                 kind="ExternalOutput" if DBG_SCR else "Internal").ap()

    stack = ExitStack()
    E = stack.enter_context

    x_sb = E(SBT("x_sb", [128, NCH * T], F32))
    s_sb = E(SBT("s_sb", [128, NCH * T], BF16))
    consts = {}
    for nm, shape, dt in CONST_SPECS:
        consts[nm] = E(SBT("c_" + nm, list(shape), dt))
    cpool = E(SBT("cpool", [128, len(CONST_VALUES)], F32))
    for i, v in enumerate(CONST_VALUES):
        nc.const_aps.register(F32, float(v), cpool[:, i:i + 1])

    def capv(v):
        i = CONST_VALUES.index(v)
        return cpool[:, i:i + 1]

    _junkv = E(SBT("hazjunk", [128, 1024], BF16))
    P.haz_junk['vector'] = _junkv
    P.haz_junk['scalar'] = _junkv
    if P.apc is not None:
        P.apc.mark_pre(_junkv)

    S32 = E(SBT("S32", [128, 512], F32))
    Sbf = E(SBT("Sbf", [128, 512], BF16))
    swcol = E(SBT("swcol", [128, 48], F32))
    stat_acc = E(SBT("stat_acc", [128, 80], F32))
    cs16 = E(SBT("cs16", [16, 2], F32))
    t22 = E(SBT("t22", [2, 2], F32))
    lnrow_box = [None]
    lnrows32 = E(SBT("lnrows32", [32, 128], F32))

    PS = [E(nc.psum_tensor("ps%d" % i, [128, 512], F32)) for i in range(6)]
    PSB = [E(nc.psum_tensor("psb%d" % i, [128, 1024], BF16)) for i in range(2)]

    wbuf = [E(SBT("wbuf%d" % i, [128, 1024], BF16)) for i in range(4)]

    def _auto_r(fn, r):
        b = getattr(fn, 'kw', {}).get('bias', None)
        if isinstance(b, float) and b in CONST_VALUES:
            return list(r) + [(cpool, CONST_VALUES.index(b))]
        return r

    def TOP(fn, r=(), w=()):
        return P.op('tensor', fn, r, w)

    def VOP(fn, r=(), w=()):
        return P.op('vector', fn, _auto_r(fn, r), w)

    def SOP(fn, r=(), w=()):
        return P.op('scalar', fn, _auto_r(fn, r), w)

    def GOP(fn, r=(), w=()):
        return P.op('gpsimd', fn, _auto_r(fn, r), w)

    def DMA(eng, fn, r=(), w=()):
        return P.op(eng, fn, r, w, dma=True)

    for nm, shape, dt in CONST_SPECS:
        DMA('sync', call('dma_start', consts[nm][:, :], din[nm]), w=[consts[nm]])
    for i, v in enumerate(CONST_VALUES):
        GOP(call('memset', cpool[:, i:i + 1], float(v)), w=[(cpool, i)])
    for ic in range(NCH):
        DMA('sync', call('dma_start', x_sb[:, ic * T:(ic + 1) * T], din["x_in"][ic]),
            w=[(x_sb, ic)])
    GOP(call('memset', S32[:, :], 0.0),
        w=[S32] + [(S32, g) for g in range(8)])
    GOP(call('memset', Sbf[:, :], 0.0), w=[(Sbf, g) for g in range(8)])

    def emit_stats(qspecs, wstg):
        for (nm, qnm, noc, n_ic, si, eps) in qspecs:
            src = din[nm]
            srcv = src.rearrange("o p i c -> o p (i c)")
            fsz = n_ic * 128
            nload = (noc * fsz) // 2048
            for b in range(nload):
                stg = wstg[b % len(wstg)]
                if fsz <= 2048:
                    opl = 2048 // fsz
                    srcs = [(stg[:, j * fsz:(j + 1) * fsz], srcv[b * opl + j])
                            for j in range(opl)]
                else:
                    hpb = fsz // 2048
                    srcs = [(stg[:, :], srcv[b // hpb][:, (b % hpb) * 2048:
                                                       (b % hpb) * 2048 + 2048])]
                for j, (dstap, inap) in enumerate(srcs):
                    DMA('sync', call('dma_start', dstap, inap), w=[(stg, j)])
                VOP(call('reduce_sum', stat_acc[:, si * 16 + b:si * 16 + b + 1],
                         stg[:, :], axis=AX.X, apply_absolute_value=True),
                    r=[(stg, j) for j in range(2)], w=[(stat_acc, (si, b))])
            TOP(call('matmul', PS[5][0:16, 0:2],
                     stat_acc[:, si * 16:si * 16 + 16], consts["onescol_f"][:, 0:2],
                     start=True, stop=True),
                r=[(stat_acc, (si, b)) for b in range(nload)] + [consts["onescol_f"]],
                w=[PS[5]])
            SOP(call('copy', cs16[0:16, 0:2], PS[5][0:16, 0:2]),
                r=[PS[5]], w=[cs16])
            TOP(call('matmul', PS[5][0:2, 4:6], cs16[0:16, 0:2],
                     consts["onescol_f"][0:16, 0:2], start=True, stop=True),
                r=[cs16, consts["onescol_f"]], w=[PS[5]])
            SOP(call('copy', t22[0:2, 0:2], PS[5][0:2, 4:6]),
                r=[PS[5]], w=[t22])
            TOP(call('matmul', PS[5][:, 8:10], consts["half2"][0:2, :],
                     t22[0:2, 0:2], start=True, stop=True),
                r=[t22, consts["half2"]], w=[PS[5]])
            SOP(call('activation', swcol[:, 8 * si:8 * si + 2], PS[5][:, 8:10],
                     AF.Relu, bias=-eps, scale=1.0 / (4096.0 * 1024.0)),
                r=[PS[5], (cpool, CONST_VALUES.index(-eps))], w=[(swcol, ('m0', si))])
            SOP(call('activation', swcol[:, 8 * si:8 * si + 2],
                     swcol[:, 8 * si:8 * si + 2], AF.Identity,
                     bias=eps, scale=1.0),
                r=[(swcol, ('m0', si)), (cpool, CONST_VALUES.index(eps))],
                w=[(swcol, ('m', si))])
            VOP(call('reciprocal', swcol[:, 8 * si + 2:8 * si + 4],
                     swcol[:, 8 * si:8 * si + 2]),
                r=[(swcol, ('m', si))], w=[(swcol, ('s', si))])
            if si < 3:
                SOP(call('mul', swcol[:, 8 * si + 4:8 * si + 6],
                         swcol[:, 8 * si:8 * si + 2], 0.5),
                    r=[(swcol, ('m', si))], w=[(swcol, ('halfm', si))])

    def emit_quant(qspecs, wstg, wsg):
        for (nm, qnm, noc, n_ic, si, eps) in qspecs:
            src, dst = din[nm], scr[qnm]
            srcv = src.rearrange("o p i c -> o p (i c)")
            dstv = dst.rearrange("o p i c -> o p (i c)")
            fsz = n_ic * 128
            nload = (noc * fsz) // 2048
            for b in range(nload):
                stg = wstg[b % len(wstg)]
                sgn = wsg[(2 * b) % len(wsg)]
                sgn2 = wsg[(2 * b + 1) % len(wsg)]
                if fsz <= 2048:
                    opl = 2048 // fsz
                    pieces = [(stg[:, j * fsz:(j + 1) * fsz], srcv[b * opl + j],
                               dstv[b * opl + j], sgn[:, j * fsz:(j + 1) * fsz])
                              for j in range(opl)]
                else:
                    hpb = fsz // 2048
                    sli = slice((b % hpb) * 2048, (b % hpb) * 2048 + 2048)
                    pieces = [(stg[:, :], srcv[b // hpb][:, sli],
                               dstv[b // hpb][:, sli], sgn[:, :])]
                for j, (dstap, inap, _, _) in enumerate(pieces):
                    DMA('sync', call('dma_start', dstap, inap), w=[(stg, j)])
                SOP(call('activation', sgn[:, :], stg[:, :], AF.Sign, bias=-0.5,
                         scale=swcol[:, 8 * si + 2:8 * si + 3]),
                    r=[(stg, j) for j in range(2)] + [(swcol, ('s', si))], w=[sgn])
                SOP(call('activation', sgn2[:, :], stg[:, :], AF.Sign, bias=0.5,
                         scale=swcol[:, 8 * si + 2:8 * si + 3]),
                    r=[(stg, j) for j in range(2)] + [(swcol, ('s', si))], w=[sgn2])
                VOP(call('tensor_add', sgn[:, :], sgn[:, :], sgn2[:, :]),
                    r=[sgn, sgn2], w=[sgn])
                for j, (_, _, outap, sgnap) in enumerate(pieces):
                    DMA('sync', call('dma_start', outap, sgnap),
                        r=[sgn], w=[(scr[qnm], b)])


    def emit_castpass(wstg, wsg):
        specs = [("wq", "wqB", 8 * 8), ("wk", "wkB", 8 * 8), ("wv", "wvB", 8 * 8),
                 ("wo", "woB", 8 * 8), ("wqkv", "qkvB", 24 * 8),
                 ("cdq", "cdqB", 8 * 4), ("cdk", "cdkB", 8 * 4), ("cdv", "cdvB", 8 * 4),
                 ("wb", "wbB", 1)]
        for idx, (nm, bnm, ntile) in enumerate(specs):
            if nm == "wb":
                srcv = din[nm].rearrange("p i c -> p (i c)")
                dstv = scr[bnm].rearrange("p i c -> p (i c)")
                blocks = [(srcv, dstv, 128)]
            else:
                srcv = din[nm].rearrange("o p i c -> o p (i c)") if nm not in ("cdq", "cdk", "cdv") else din[nm].rearrange("o p k c -> o p (k c)")
                dstv = scr[bnm].rearrange("o p i c -> o p (i c)") if nm not in ("cdq", "cdk", "cdv") else scr[bnm].rearrange("o p k c -> o p (k c)")
                noc = srcv.shape[0]
                fsz = srcv.shape[2]
                blocks = [(srcv[o], dstv[o], fsz) for o in range(noc)]
            for bi, (sap, dap, fsz) in enumerate(blocks):
                par = bi % 2
                stg = wstg[par]
                sg = wsg[par]
                DMA('sync', call('dma_start', stg[:, 0:fsz], sap), w=[(stg, 'cp')])
                eng = VOP if bi % 2 == 0 else SOP
                if bi % 2 == 0:
                    VOP(call('tensor_copy', sg[:, 0:fsz], stg[:, 0:fsz]),
                        r=[(stg, 'cp')], w=[(sg, 'cp')])
                else:
                    SOP(call('copy', sg[:, 0:fsz], stg[:, 0:fsz]),
                        r=[(stg, 'cp')], w=[(sg, 'cp')])
                DMA('sync', call('dma_start', dap, sg[:, 0:fsz]),
                    r=[(sg, 'cp')], w=[(scr[bnm], bi)])

    def emit_ln(xbf, x2stg, xn_dst, pF, xn_dram=None):
        lnrow = lnrow_box[0]
        base_par = P.paranoid
        def sec(f):
            P.paranoid = base_par and not f
        nblk = T // 128
        sec(LN_A_RELAX)
        for b in range(nblk):
            x2 = x2stg[b % len(x2stg)]
            for ic in range(NCH):
                VOP(call('tensor_mul', x2[:, ic * 128:(ic + 1) * 128],
                         xbf[:, ic * T + b * 128: ic * T + b * 128 + 128],
                         xbf[:, ic * T + b * 128: ic * T + b * 128 + 128]),
                    r=[(xbf, ic)], w=[(x2, ic)])
            for ic in range(NCH):
                TOP(call('matmul', PS[0][:, b:b + 1],
                         xbf[:, ic * T + b * 128: ic * T + b * 128 + 128],
                         consts["onescol"][:, :], start=(ic == 0), stop=False),
                    r=[(xbf, ic), consts["onescol"]], w=[PS[0]])
                TOP(call('matmul', PS[0][:, 16 + b:16 + b + 1],
                         x2[:, ic * 128:(ic + 1) * 128],
                         consts["onescol"][:, :], start=False, stop=(ic == NCH - 1)),
                    r=[(x2, ic), consts["onescol"]], w=[PS[0]])
        sec(LN_B_RELAX)
        VOP(call('tensor_scalar', pF[:, 0:32], PS[0][:, 0:32], 1.0 / D, None, ALU.mult),
            r=[PS[0]], w=[(pF, 'a')])
        VOP(call('tensor_tensor', pF[:, 32:48], pF[:, 0:16],
                 pF[:, 0:16], ALU.mult),
            r=[(pF, 'a')], w=[(pF, 'v')])
        VOP(call('tensor_tensor', pF[:, 32:48], pF[:, 16:32],
                 pF[:, 32:48], ALU.subtract),
            r=[(pF, 'a'), (pF, 'v')], w=[(pF, 'v')])
        if DUMP4 and PHASES == 4 and xn_dram is not None:
            DMA('sync', call('dma_start', x_out[3][:, 0:16], pF[:, 32:48]),
                r=[(pF, 'v')], w=[(x_out, 'f3a')])
        SOP(call('activation', pF[:, 48:64], pF[:, 32:48], AF.Ln, bias=1e-5, scale=1.0),
            r=[(pF, 'v')], w=[(pF, 'r')])
        SOP(call('activation', pF[:, 48:64], pF[:, 48:64], AF.Exp, bias=0.0,
                 scale=capv(-0.5)),
            r=[(pF, 'r'), (cpool, CONST_VALUES.index(-0.5))], w=[(pF, 'r')])
        if DUMP4 and PHASES == 4 and xn_dram is not None:
            DMA('sync', call('dma_start', x_out[3][:, 16:32], pF[:, 48:64]),
                r=[(pF, 'r')], w=[(x_out, 'f3b')])
        VOP(call('tensor_copy', pF[:, 64:80], pF[:, 0:16]),
            r=[(pF, 'a')], w=[(pF, 'p')])
        VOP(call('tensor_copy', pF[:, 80:96], pF[:, 48:64]),
            r=[(pF, 'r')], w=[(pF, 'p')])
        if DUMP4 and PHASES == 4 and xn_dram is not None:
            DMA('sync', call('dma_start', x_out[3][:, 32:64], pF[:, 64:96]),
                r=[(pF, 'p')], w=[(x_out, 'f3c')])
        sec(LN_C_RELAX)
        TOP(call('transpose', PS[1][0:32, 0:128], pF[:, 64:96], consts["identf"][:, :]),
            r=[(pF, 'p'), consts["identf"]], w=[PS[1]])
        VOP(call('tensor_copy', lnrows32[:, :], PS[1][0:32, 0:128]),
            r=[PS[1]], w=[lnrows32])
        DMA('sync', call('dma_start', scr['lnflat'].rearrange("(j t) -> j t", t=128),
                         lnrows32[:, :]),
            r=[lnrows32], w=[scr['lnflat']])
        sec(LN_D_RELAX)
        for tt in range(4):
            DMA('sync', call('dma_start', lnrow[0:1, 0:512],
                             scr['lnflat'][tt * 512:(tt + 1) * 512].rearrange("(o f) -> o f", o=1)),
                r=[scr['lnflat']], w=[(lnrow, 'm')])
            DMA('sync', call('dma_start', lnrow[0:1, 512:1024],
                             scr['lnflat'][2048 + tt * 512:2048 + (tt + 1) * 512].rearrange("(o f) -> o f", o=1)),
                r=[scr['lnflat']], w=[(lnrow, 'r')])
            for b4 in range(4):
                TOP(call('matmul', PS[2][:, b4 * 128:(b4 + 1) * 128],
                         consts["onesrow_f"][0:1, :],
                         lnrow[0:1, b4 * 128:(b4 + 1) * 128],
                         start=True, stop=True),
                    r=[(lnrow, 'm'), consts["onesrow_f"]], w=[PS[2]])
                TOP(call('matmul', PS[3][:, b4 * 128:(b4 + 1) * 128],
                         consts["onesrow_f"][0:1, :],
                         lnrow[0:1, 512 + b4 * 128:512 + (b4 + 1) * 128],
                         start=True, stop=True),
                    r=[(lnrow, 'r'), consts["onesrow_f"]], w=[PS[3]])
            for ic in range(NCH):
                VOP(call('tensor_tensor', pF[:, 32:544],
                         x_sb[:, ic * T + tt * SP: ic * T + tt * SP + SP],
                         PS[2][:, 0:512], ALU.subtract),
                    r=[(x_sb, ic), PS[2]],
                    w=[(pF, 'sub'), (pF, 'v'), (pF, 'r'), (pF, 'p')])
                VOP(call('tensor_tensor',
                         xn_dst[:, ic * T + tt * SP: ic * T + tt * SP + SP],
                         pF[:, 32:544], PS[3][:, 0:512], ALU.mult),
                    r=[(pF, 'sub'), PS[3]], w=[(xn_dst, ic)])
                if xn_dram is not None:
                    DMA('sync', call('dma_start',
                                     xn_dram[ic][:, tt * SP:(tt + 1) * SP],
                                     xn_dst[:, ic * T + tt * SP: ic * T + tt * SP + SP]),
                        r=[(xn_dst, ic)], w=[(scr['xn_d'], (ic, tt))])

    def emit_a1(xn, bb):
        cdbuf = bb['cdbuf']
        wbb = bb['wbb']
        DMA('sync', call('dma_start', wbb[:, :],
                         scr['wbB'].rearrange("p i c -> p (i c)")),
            r=[(scr['wbB'], 0)], w=[wbb])
        bpre, bbf, nbbf = bb['bpre'], bb['bbf'], bb['nbbf']
        for tt in range(4):
            for ic in range(NCH):
                TOP(call('matmul', PS[4][0:16, 0:512], wbb[:, ic * 16:(ic + 1) * 16],
                         xn[:, ic * T + tt * SP: ic * T + tt * SP + SP],
                         start=(ic == 0), stop=(ic == NCH - 1)),
                    r=[(xn, ic), wbb], w=[PS[4]])
            SOP(call('activation', bpre[0:16, 0:512], PS[4][0:16, 0:512], AF.Exp,
                     bias=0.0, scale=capv(-1.0)[0:16, 0:1]),
                r=[PS[4], (cpool, CONST_VALUES.index(-1.0))], w=[bpre])
            SOP(call('activation', bpre[0:16, 0:512], bpre[0:16, 0:512], AF.Ln,
                     bias=capv(1.0)[0:16, 0:1], scale=1.0),
                r=[bpre, (cpool, CONST_VALUES.index(1.0))], w=[bpre])
            SOP(call('activation', bbf[0:16, tt * SP:(tt + 1) * SP], bpre[0:16, 0:512],
                     AF.Exp, bias=0.0, scale=capv(-1.0)[0:16, 0:1]),
                r=[bpre], w=[(bbf, tt)])
            VOP(call('tensor_scalar', nbbf[0:16, tt * SP:(tt + 1) * SP],
                     bbf[0:16, tt * SP:(tt + 1) * SP], -1.0, None, ALU.mult),
                r=[(bbf, tt)], w=[(nbbf, tt)])

        bbf2, nbbf2 = bb['bbf2'], bb['nbbf2']
        DMA('sync', call('dma_start', scr['bflat'], bbf[0:16, :]),
            r=[(bbf, tt) for tt in range(4)], w=[scr['bflat']])
        DMA('sync', call('dma_start', scr['nbflat'], nbbf[0:16, :]),
            r=[(nbbf, tt) for tt in range(4)], w=[scr['nbflat']])
        for tname, wnm, cdnm in [('q', 'wqB', 'cdqB'), ('k', 'wkB', 'cdkB'),
                                 ('v', 'wvB', 'cdvB')]:
            wsrc = scr[wnm].rearrange("o p i c -> o p (i c)")
            cdsrc = scr[cdnm].rearrange("o p k c -> o p (k c)")
            for oc in range(NCH):
                par = oc % 2
                wb_ = wbuf[par]
                raw = bb['raw'][par]
                sil = bb['sil'][par]
                cd = cdbuf[par]
                DMA('sync', call('dma_start', wb_[:, :], wsrc[oc]),
                    r=[(scr[wnm], oc)], w=[wb_])
                DMA('sync', call('dma_start', cd[:, :], cdsrc[oc]),
                    r=[(scr[cdnm], oc)], w=[cd])
                psp = PS[par]
                psc = PS[2 + par]
                for tt in range(4):
                    for ic in range(NCH):
                        TOP(call('matmul', psp[:, 0:512],
                                 wb_[:, ic * 128:(ic + 1) * 128],
                                 xn[:, ic * T + tt * SP: ic * T + tt * SP + SP],
                                 start=(ic == 0), stop=(ic == NCH - 1)),
                            r=[(xn, ic), wb_], w=[psp])
                    VOP(call('tensor_copy', raw[:, 3 + tt * SP: 3 + tt * SP + SP],
                             psp[:, 0:512]),
                        r=[psp], w=[(raw, tt)])
                for tt in range(4):
                    rdeps = [(raw, tt), cd]
                    if tt > 0:
                        rdeps.append((raw, tt - 1))
                    else:
                        rdeps.append((raw, 'halo'))
                    for k in range(4):
                        TOP(call('matmul', psc[:, 0:512], cd[:, k * 128:(k + 1) * 128],
                                 raw[:, k + tt * SP: k + tt * SP + SP],
                                 start=(k == 0), stop=(k == 3)),
                            r=rdeps, w=[psc])
                    SOP(call('activation', sil[:, tt * SP:(tt + 1) * SP], psc[:, 0:512],
                             AF.Silu, bias=0.0, scale=1.0),
                        r=[psc], w=[(sil, tt)])
                if DBG_SCR and tname == 'v' and oc == 3:
                    DMA('sync', call('dma_start', dbg_raw, raw[:, :]),
                        r=[(raw, tt) for tt in range(4)] + [(raw, 'halo')],
                        w=[(dbg_raw, 0)])
                    DMA('sync', call('dma_start', dbg_sil, sil[:, :]),
                        r=[(sil, tt) for tt in range(4)], w=[(dbg_sil, 0)])
                    P.op('vector', call('engine_nop'),
                         reads=[(dbg_raw, 0), (dbg_sil, 0)])
                if tname == 'v':
                    ob = bb['ob1'][par]
                    DMA('sync', call('dma_start', bbf2[0:2, :],
                                     scr['bflat'][2 * oc:2 * oc + 2, :]),
                        r=[scr['bflat']], w=[bbf2])
                    for tt in range(4):
                        TOP(call('matmul', PS[4][:, 0:512], consts["bc2"][0:2, :],
                                 bbf2[0:2, tt * SP:(tt + 1) * SP],
                                 start=True, stop=True),
                            r=[bbf2, consts["bc2"]], w=[PS[4]])
                        VOP(call('tensor_tensor', ob[:, tt * SP:(tt + 1) * SP],
                                 PS[4][:, 0:512], sil[:, tt * SP:(tt + 1) * SP],
                                 ALU.mult),
                            r=[PS[4], (sil, tt)], w=[(ob, tt)])
                    DMA('sync', call('dma_start', scr['vb'][oc], ob[:, :]),
                        r=[(ob, tt) for tt in range(4)], w=[(scr['vb'], oc)])
                else:
                    sq = bb['sq'][0]
                    lnb = bb['lnb'][0]
                    rr = bb['rr'][par]
                    ob = bb['ob1'][par]
                    lim = 1.25e5 if tname == 'q' else 1e6
                    for tt in range(4):
                        VOP(call('tensor_mul', sq[:, 0:512],
                                 sil[:, tt * SP:(tt + 1) * SP],
                                 sil[:, tt * SP:(tt + 1) * SP]),
                            r=[(sil, tt)], w=[sq])
                        TOP(call('matmul', PS[4][:, 0:512], consts["ind2"][:, :],
                                 sq[:, 0:512], start=True, stop=True),
                            r=[sq, consts["ind2"]], w=[PS[4]])
                        SOP(call('activation', lnb[:, 0:512], PS[4][:, 0:512], AF.Ln,
                                 bias=1e-30, scale=1.0),
                            r=[PS[4], (cpool, CONST_VALUES.index(1e-30))], w=[lnb])
                        if tname == 'q':
                            SOP(call('activation', rr[:, 0:512], lnb[:, 0:512], AF.Exp,
                                     bias=capv(LN8), scale=capv(-0.5)),
                                r=[lnb, (cpool, CONST_VALUES.index(LN8))], w=[rr])
                        else:
                            SOP(call('activation', rr[:, 0:512], lnb[:, 0:512], AF.Exp,
                                     bias=0.0, scale=capv(-0.5)),
                                r=[lnb], w=[rr])
                        VOP(call('scalar_tensor_tensor', ob[:, tt * SP:(tt + 1) * SP],
                                 rr[:, 0:512], lim, sil[:, tt * SP:(tt + 1) * SP],
                                 ALU.min, ALU.mult),
                            r=[rr, (sil, tt)], w=[(ob, tt)])
                    dnm = 'qh' if tname == 'q' else 'kh'
                    DMA('sync', call('dma_start', scr[dnm][oc], ob[:, :]),
                        r=[(ob, tt) for tt in range(4)], w=[(scr[dnm], oc)])
                    if tname == 'k':
                        ob2 = bb['ob2'][par]
                        DMA('sync', call('dma_start', nbbf2[0:2, :],
                                         scr['nbflat'][2 * oc:2 * oc + 2, :]),
                            r=[scr['nbflat']], w=[nbbf2])
                        for tt in range(4):
                            TOP(call('matmul', PS[4][:, 0:512], consts["bc2"][0:2, :],
                                     nbbf2[0:2, tt * SP:(tt + 1) * SP],
                                     start=True, stop=True),
                                r=[nbbf2, consts["bc2"]], w=[PS[4]])
                            VOP(call('tensor_tensor', ob2[:, tt * SP:(tt + 1) * SP],
                                     PS[4][:, 0:512], ob[:, tt * SP:(tt + 1) * SP],
                                     ALU.mult),
                                r=[PS[4], (ob, tt)], w=[(ob2, tt)])
                        DMA('sync', call('dma_start', scr['kbn'][oc], ob2[:, :]),
                            r=[(ob2, tt) for tt in range(4)], w=[(scr['kbn'], oc)])

    def emit_a2(bb):
        def sec(name):
            if RELAX is not None:
                P.paranoid = name not in RELAX
        spq, spk, spkbn, spvb = bb['spq'], bb['spk'], bb['spkbn'], bb['spvb']
        pow_sb = bb['pow']
        Msb, Dtsb = bb['Msb'], bb['Dtsb']
        Xf, Xb, vbtk = bb['Xf'], bb['Xb'], bb['vbtk']
        otmn, o2s, ssd, rno, ofm = bb['otmn'], bb['o2s'], bb['ssd'], bb['rno'], bb['ofm']

        def ldspan(sp, par):
            sec('ldspan')
            for nm, buf in [('qh', spq), ('kh', spk), ('kbn', spkbn), ('vb', spvb)]:
                bufp = buf[par]
                for oc in range(NCH):
                    DMA('sync', call('dma_start', bufp[:, oc * SP:(oc + 1) * SP],
                                     scr[nm][oc][:, sp * SP:(sp + 1) * SP]),
                        r=[(scr[nm], oc)], w=[bufp])

        ldspan(0, 0)
        for sp in range(NSP):
            par = sp % 2
            if sp + 1 < NSP:
                ldspan(sp + 1, 1 - par)
            q_, k_, kbn_, vb_ = spq[par], spk[par], spkbn[par], spvb[par]
            for cb in range(SP // C):
                for w in range(2):
                    wp = w
                    gs = [4 * w + i for i in range(4)]
                    Ms, Dts, Xfw, Xbw, vk = Msb[0], Dtsb[0], Xf[wp], Xb[wp], vbtk[0]

                    def sl(buf, g, h):
                        return buf[64 * h:64 * h + 64,
                                   g * SP + cb * C: g * SP + cb * C + C]

                    sec('M')
                    for gi, g in enumerate(gs):
                        bank = PS[0] if gi < 2 else PS[1]
                        base = (gi % 2) * 256
                        for h in range(2):
                            TOP(call('matmul',
                                     bank[64 * h:64 * h + 64,
                                          base + 64 * h: base + 64 * h + 64],
                                     sl(kbn_, g, h), sl(k_, g, h),
                                     start=True, stop=True),
                                r=[kbn_, k_], w=[bank])
                            TOP(call('matmul',
                                     bank[64 * h:64 * h + 64,
                                          base + 128 + 64 * h: base + 128 + 64 * h + 64],
                                     sl(k_, g, h), sl(kbn_, g, h),
                                     start=True, stop=True),
                                r=[kbn_, k_], w=[bank])
                    for bi, bank in enumerate((PS[0], PS[1])):
                        VOP(call('scalar_tensor_tensor', Ms[:, bi * 512:(bi + 1) * 512],
                                 bank[:, 0:512], 1.0, consts["mmask2"][:, :],
                                 ALU.mult, ALU.mult),
                            r=[bank, consts["mmask2"]], w=[(Ms, bi)])
                    sec('Dt')
                    for gi, g in enumerate(gs):
                        for h in range(2):
                            TOP(call('matmul',
                                     PS[2][64 * h:64 * h + 64,
                                           gi * 128 + 64 * h: gi * 128 + 64 * h + 64],
                                     sl(k_, g, h), sl(q_, g, h), start=True, stop=True),
                                r=[k_, q_], w=[PS[2]])
                    VOP(call('scalar_tensor_tensor', Dts[:, 0:512], PS[2][:, 0:512],
                             1.0, consts["tiumask4"][:, :], ALU.mult, ALU.mult),
                        r=[PS[2], consts["tiumask4"]], w=[Dts])
                    sec('sq')
                    for lev in range(1, NLEV + 1):
                        for gi in range(4):
                            bank = PS[0] if gi < 2 else PS[1]
                            base = (gi % 2) * 256
                            if lev == 1:
                                srcR = Ms[:, gi * 256:gi * 256 + 128]
                                srcC = Ms[:, gi * 256 + 128:gi * 256 + 256]
                                rkey = (Ms, gi // 2)
                            else:
                                pv = pow_sb[lev - 2]
                                srcR = pv[:, gi * 256:gi * 256 + 128]
                                srcC = pv[:, gi * 256 + 128:gi * 256 + 256]
                                rkey = (pv, gi // 2)
                            if lev < NLEV:
                                TOP(call('matmul', bank[:, base:base + 128], srcC, srcR,
                                         start=True, stop=True),
                                    r=[rkey], w=[bank])
                            TOP(call('matmul', bank[:, base + 128:base + 256],
                                     srcR, srcC, start=True, stop=True),
                                r=[rkey], w=[bank])
                        pv = pow_sb[lev - 1]
                        for bi, bank in enumerate((PS[0], PS[1])):
                            VOP(call('tensor_copy', pv[:, bi * 512:(bi + 1) * 512],
                                     bank[:, 0:512]),
                                r=[bank], w=[(pv, bi)])
                    sec('trans')
                    for gi, g in enumerate(gs):
                        for h in range(2):
                            idh = consts["ident"][64 * h:64 * h + 64,
                                                  64 * h:64 * h + 64]
                            TOP(call('transpose',
                                     PSB[0][64 * h:64 * h + 64, gi * 64:gi * 64 + 64],
                                     sl(vb_, g, h), idh),
                                r=[vb_, consts["ident"]], w=[PSB[0]])
                            TOP(call('transpose',
                                     PSB[0][64 * h:64 * h + 64,
                                            256 + gi * 64: 256 + gi * 64 + 64],
                                     sl(k_, g, h), idh),
                                r=[k_, consts["ident"]], w=[PSB[0]])
                    VOP(call('tensor_copy', vk[:, 0:512], PSB[0][:, 0:512]),
                        r=[PSB[0]], w=[vk])
                    sec('R')
                    for gi, g in enumerate(gs):
                        for h in range(2):
                            TOP(call('matmul',
                                     PS[3][64 * h:64 * h + 64, gi * 64:gi * 64 + 64],
                                     sl(kbn_, g, h),
                                     Sbf[64 * h:64 * h + 64, g * 64:g * 64 + 64],
                                     start=True, stop=True),
                                r=[kbn_, (Sbf, g)], w=[PS[3]])
                    VOP(call('tensor_tensor', Xfw[:, 0:256], PS[3][:, 0:256],
                             vk[:, 0:256], ALU.add),
                        r=[PS[3], vk], w=[Xfw])
                    SOP(call('copy', Xbw[:, 0:256], Xfw[:, 0:256]),
                        r=[Xfw], w=[Xbw])
                    sec('app')
                    for lev in list(range(NLEV, 0, -1)) + [0]:
                        for gi in range(4):
                            if lev == 0:
                                lhs = Ms[:, gi * 256 + 128:gi * 256 + 256]
                                rkey = (Ms, gi // 2)
                            else:
                                pv = pow_sb[lev - 1]
                                lhs = pv[:, gi * 256 + 128:gi * 256 + 256]
                                rkey = (pv, gi // 2)
                            TOP(call('matmul', PS[3][:, 256 + gi * 64: 256 + gi * 64 + 64],
                                     lhs, Xbw[:, gi * 64:gi * 64 + 64],
                                     start=True, stop=True),
                                r=[rkey, Xbw], w=[PS[3]])
                        VOP(call('tensor_tensor', Xfw[:, 0:256], Xfw[:, 0:256],
                                 PS[3][:, 256:512], ALU.add),
                            r=[PS[3], Xfw], w=[Xfw])
                        SOP(call('copy', Xbw[:, 0:256], Xfw[:, 0:256]),
                            r=[Xfw], w=[Xbw])
                    sec('O')
                    for gi, g in enumerate(gs):
                        for h in range(2):
                            TOP(call('matmul',
                                     PS[4][64 * h:64 * h + 64, gi * 64:gi * 64 + 64],
                                     sl(q_, g, h),
                                     Sbf[64 * h:64 * h + 64, g * 64:g * 64 + 64],
                                     start=True, stop=False),
                                r=[q_, (Sbf, g)], w=[PS[4]])
                        TOP(call('matmul', PS[4][:, gi * 64:gi * 64 + 64],
                                 Dts[:, gi * 128:gi * 128 + 128],
                                 Xbw[:, gi * 64:gi * 64 + 64], start=False, stop=True),
                            r=[Dts, Xbw], w=[PS[4]])
                    sec('onorm')
                    for gi in range(4):
                        SOP(call('activation', o2s[:, 0:64],
                                 PS[4][:, gi * 64:gi * 64 + 64], AF.Square,
                                 bias=0.0, scale=1.0,
                                 accum_out=ssd[:, 4 * w + gi:4 * w + gi + 1]),
                            r=[PS[4]], w=[(ssd, 4 * w + gi), o2s])
                    SOP(call('activation', rno[:, 4 * w:4 * w + 4],
                             ssd[:, 4 * w:4 * w + 4], AF.Ln, bias=1e-6,
                             scale=1.0 / 64.0),
                        r=[(ssd, 4 * w + gi) for gi in range(4)] +
                          [(cpool, CONST_VALUES.index(1e-6))],
                        w=[(rno, w)])
                    SOP(call('activation', rno[:, 4 * w:4 * w + 4],
                             rno[:, 4 * w:4 * w + 4], AF.Exp, bias=0.0,
                             scale=capv(-0.5)),
                        r=[(rno, w)], w=[(rno, w)])
                    for gi in range(4):
                        VOP(call('scalar_tensor_tensor', otmn[:, gi * 64:gi * 64 + 64],
                                 PS[4][:, gi * 64:gi * 64 + 64],
                                 rno[:, 4 * w + gi:4 * w + gi + 1],
                                 consts["onw"][:, :], ALU.mult, ALU.mult),
                            r=[PS[4], (rno, w), consts["onw"]], w=[(otmn, gi)])
                    sec('otrans')
                    for gi, g in enumerate(gs):
                        for h in range(2):
                            idh = consts["ident"][64 * h:64 * h + 64,
                                                  64 * h:64 * h + 64]
                            TOP(call('transpose',
                                     PSB[1][64 * h:64 * h + 64, gi * 64:gi * 64 + 64],
                                     otmn[64 * h:64 * h + 64, gi * 64:gi * 64 + 64],
                                     idh),
                                r=[(otmn, gi), consts["ident"]], w=[PSB[1]])
                        VOP(call('tensor_copy',
                                 ofm[:, g * SP + cb * C: g * SP + cb * C + C],
                                 PSB[1][:, gi * 64:gi * 64 + 64]),
                            r=[PSB[1]], w=[(ofm, g)])
                    sec('supd')
                    for gi, g in enumerate(gs):
                        for h in range(2):
                            TOP(call('matmul',
                                     PS[5][64 * h:64 * h + 64, gi * 64:gi * 64 + 64],
                                     vk[64 * h:64 + 64 * h, 256 + gi * 64:256 + gi * 64 + 64],
                                     Xbw[64 * h:64 * h + 64, gi * 64:gi * 64 + 64],
                                     start=True, stop=True),
                                r=[vk, Xbw], w=[PS[5]])
                    VOP(call('tensor_tensor', S32[:, 256 * w:256 * w + 256],
                             S32[:, 256 * w:256 * w + 256], PS[5][:, 0:256], ALU.add),
                        r=[PS[5]] + [(S32, g) for g in gs],
                        w=[(S32, g) for g in gs])
                    SOP(call('copy', Sbf[:, 256 * w:256 * w + 256],
                             S32[:, 256 * w:256 * w + 256]),
                        r=[(S32, g) for g in gs], w=[(Sbf, g) for g in gs])
            sec('wo')
            wosrc = scr['woB'].rearrange("o p i c -> o p (i c)")
            for oc in range(NCH):
                wb_ = wbuf[2 + oc % 2]
                DMA('sync', call('dma_start', wb_[:, :], wosrc[oc]),
                    r=[(scr['woB'], oc)], w=[wb_])
                for ic in range(NCH):
                    TOP(call('matmul', PS[2][:, 0:512],
                             wb_[:, ic * 128:(ic + 1) * 128],
                             ofm[:, ic * SP:(ic + 1) * SP], start=(ic == 0),
                             stop=(ic == NCH - 1)),
                        r=[wb_, (ofm, ic)], w=[PS[2]])
                VOP(call('tensor_copy',
                         s_sb[:, oc * T + sp * SP: oc * T + sp * SP + SP],
                         PS[2][:, 0:512]),
                    r=[PS[2]], w=[(s_sb, (oc, sp))])

    def emit_l0b(bb):
        wbufD = bb['wbufD']
        xnt = bb['xnt']
        xq, hraw = bb['xq'], bb['hraw']
        mp, mn, mx4 = bb['mpos'], bb['mneg'], bb['mx4']
        mrows, mrow1 = bb['mrows'], bb['mrow1']
        sxb, sxib = bb['sxb'], bb['sxib']
        tq, gs_, tu = bb['tq'], bb['gs'], bb['tu']
        gq = tq
        wgv = scr['wgq'].rearrange("o p i c -> o p (i c)")
        wuv = scr['wuq'].rearrange("o p i c -> o p (i c)")
        wdv = scr['wdq'].rearrange("o p i c -> o p (i c)")

        def quant_scales():
            VOP(call('tensor_scalar', mn[:, 0:512], mn[:, 0:512], -1.0, None, ALU.mult),
                r=[mn], w=[mn])
            VOP(call('tensor_max', mp[:, 0:512], mp[:, 0:512], mn[:, 0:512]),
                r=[mp, mn], w=[mp])
            for b4 in range(4):
                TOP(call('transpose', PS[3][:, b4 * 128:(b4 + 1) * 128],
                         mp[:, b4 * 128:(b4 + 1) * 128], consts["identf"][:, :]),
                    r=[mp, consts["identf"]], w=[PS[3]])
            for b4 in range(4):
                VOP(call('reduce_max', mx4[:, b4:b4 + 1],
                         PS[3][:, b4 * 128:(b4 + 1) * 128], axis=AX.X),
                    r=[PS[3]], w=[(mx4, 'raw')])
            SOP(call('activation', mx4[:, 4:8], mx4[:, 0:4], AF.Relu,
                     bias=-1e-5, scale=1.0),
                r=[(mx4, 'raw'), (cpool, CONST_VALUES.index(-1e-5))], w=[(mx4, 'c0')])
            SOP(call('activation', mx4[:, 4:8], mx4[:, 4:8], AF.Identity,
                     bias=1e-5, scale=1.0),
                r=[(mx4, 'c0'), (cpool, CONST_VALUES.index(1e-5))], w=[(mx4, 'mc')])
            SOP(call('mul', mx4[:, 8:12], mx4[:, 4:8], 1.0 / 127.0),
                r=[(mx4, 'mc')], w=[(mx4, 'inv')])
            VOP(call('reciprocal', mx4[:, 12:16], mx4[:, 4:8]),
                r=[(mx4, 'mc')], w=[(mx4, 'rc')])
            SOP(call('mul', mx4[:, 12:16], mx4[:, 12:16], 127.0),
                r=[(mx4, 'rc')], w=[(mx4, 'sx')])
            TOP(call('transpose', PS[3][0:8, 0:128], mx4[:, 8:16], consts["identf"][:, :]),
                r=[(mx4, 'inv'), (mx4, 'sx'), consts["identf"]], w=[PS[3]])
            VOP(call('tensor_copy', mrows[:, :], PS[3][0:8, 0:128]),
                r=[PS[3]], w=[mrows])
            DMA('sync', call('dma_start',
                             scr['lnflat'][0:1024].rearrange("(j t) -> j t", t=128),
                             mrows[:, :]),
                r=[mrows], w=[scr['lnflat']])
            DMA('sync', call('dma_start', mrow1[0:1, 0:1024],
                             scr['lnflat'][0:1024].rearrange("(o f) -> o f", o=1)),
                r=[scr['lnflat']], w=[mrow1])
            for b4 in range(4):
                TOP(call('matmul', PS[3][:, b4 * 128:(b4 + 1) * 128],
                         consts["onesrow_f"][0:1, :],
                         mrow1[0:1, 512 + b4 * 128: 512 + (b4 + 1) * 128],
                         start=True, stop=True),
                    r=[mrow1, consts["onesrow_f"]], w=[PS[3]])
            VOP(call('tensor_copy', sxb[:, 0:512], PS[3][:, 0:512]),
                r=[PS[3]], w=[sxb])
            for b4 in range(4):
                TOP(call('matmul', PS[3][:, b4 * 128:(b4 + 1) * 128],
                         consts["onesrow_f"][0:1, :],
                         mrow1[0:1, b4 * 128:(b4 + 1) * 128], start=True, stop=True),
                    r=[mrow1, consts["onesrow_f"]], w=[PS[3]])
            VOP(call('tensor_copy', sxib[:, 0:512], PS[3][:, 0:512]),
                r=[PS[3]], w=[sxib])

        for tt in range(4):
            xt = xnt[0]
            for ic in range(NCH):
                DMA('sync', call('dma_start', xt[:, ic * SP:(ic + 1) * SP],
                                 scr['xn_d'][ic][:, tt * SP:(tt + 1) * SP]),
                    r=[(scr['xn_d'], (ic, tt))], w=[(xt, ic)])
            VOP(call('tensor_max', mp[:, 0:512], xt[:, 0:512], xt[:, 512:1024]),
                r=[(xt, 0), (xt, 1)], w=[mp])
            VOP(call('tensor_tensor', mn[:, 0:512], xt[:, 0:512], xt[:, 512:1024],
                     ALU.min),
                r=[(xt, 0), (xt, 1)], w=[mn])
            for ic in range(2, NCH):
                VOP(call('tensor_max', mp[:, 0:512], mp[:, 0:512],
                         xt[:, ic * SP:(ic + 1) * SP]),
                    r=[mp, (xt, ic)], w=[mp])
                VOP(call('tensor_tensor', mn[:, 0:512], mn[:, 0:512],
                         xt[:, ic * SP:(ic + 1) * SP], ALU.min),
                    r=[mn, (xt, ic)], w=[mn])
            quant_scales()
            for ic in range(NCH):
                VOP(call('tensor_tensor', tq[:, 0:512], xt[:, ic * SP:(ic + 1) * SP],
                         sxb[:, 0:512], ALU.mult),
                    r=[(xt, ic), sxb], w=[tq])
                VOP(call('tensor_scalar', xq[:, ic * SP:(ic + 1) * SP], tq[:, 0:512],
                         CRND, CRND, ALU.add, ALU.subtract),
                    r=[tq], w=[(xq, ic)])
            for oc in range(32):
                par = oc % 2
                wg_, wu_ = wbuf[par], wbuf[2 + par]
                DMA('sync', call('dma_start', wg_[:, :], wgv[oc]), w=[wg_])
                DMA('sync', call('dma_start', wu_[:, :], wuv[oc]), w=[wu_])
                psg, psu = PS[par], PS[2 + par]
                for ic in range(NCH):
                    TOP(call('matmul', psg[:, 0:512], wg_[:, ic * 128:(ic + 1) * 128],
                             xq[:, ic * SP:(ic + 1) * SP], start=(ic == 0),
                             stop=(ic == NCH - 1)),
                        r=[wg_, (xq, ic)], w=[psg])
                    TOP(call('matmul', psu[:, 0:512], wu_[:, ic * 128:(ic + 1) * 128],
                             xq[:, ic * SP:(ic + 1) * SP], start=(ic == 0),
                             stop=(ic == NCH - 1)),
                        r=[wu_, (xq, ic)], w=[psu])
                VOP(call('tensor_tensor', tq[:, 0:512], psg[:, 0:512], sxib[:, 0:512],
                         ALU.mult),
                    r=[psg, sxib], w=[tq])
                SOP(call('activation', gs_[:, 0:512], tq[:, 0:512], AF.Silu,
                         bias=0.0, scale=swcol[:, 4:5]),
                    r=[tq, (swcol, ('halfm', 0))], w=[gs_])
                SOP(call('activation', tu[:, 0:512], psu[:, 0:512], AF.Identity,
                         bias=0.0, scale=swcol[:, 12:13]),
                    r=[psu, (swcol, ('halfm', 1))], w=[tu])
                VOP(call('tensor_tensor', tu[:, 0:512], tu[:, 0:512], sxib[:, 0:512],
                         ALU.mult),
                    r=[tu, sxib], w=[tu])
                VOP(call('tensor_tensor', hraw[:, oc * 512:(oc + 1) * 512],
                         gs_[:, 0:512], tu[:, 0:512], ALU.mult),
                    r=[gs_, tu], w=[(hraw, oc)])
            VOP(call('tensor_max', mp[:, 0:512], hraw[:, 0:512], hraw[:, 512:1024]),
                r=[(hraw, 0), (hraw, 1)], w=[mp])
            VOP(call('tensor_tensor', mn[:, 0:512], hraw[:, 0:512], hraw[:, 512:1024],
                     ALU.min),
                r=[(hraw, 0), (hraw, 1)], w=[mn])
            for oc in range(2, 32):
                VOP(call('tensor_max', mp[:, 0:512], mp[:, 0:512],
                         hraw[:, oc * 512:(oc + 1) * 512]),
                    r=[mp, (hraw, oc)], w=[mp])
                VOP(call('tensor_tensor', mn[:, 0:512], mn[:, 0:512],
                         hraw[:, oc * 512:(oc + 1) * 512], ALU.min),
                    r=[mn, (hraw, oc)], w=[mn])
            quant_scales()
            for oc in range(32):
                VOP(call('tensor_tensor', gq[:, 0:512],
                         hraw[:, oc * 512:(oc + 1) * 512], sxb[:, 0:512], ALU.mult),
                    r=[(hraw, oc), sxb], w=[gq])
                VOP(call('tensor_scalar', hraw[:, oc * 512:(oc + 1) * 512], gq[:, 0:512],
                         CRND, CRND, ALU.add, ALU.subtract),
                    r=[gq], w=[(hraw, oc)])
            for oc2 in range(NCH):
                par = oc2 % 2
                wd_ = wbufD[par]
                DMA('sync', call('dma_start', wd_[:, :], wdv[oc2]), w=[wd_])
                psd = PS[4 + par]
                for ic in range(32):
                    TOP(call('matmul', psd[:, 0:512], wd_[:, ic * 128:(ic + 1) * 128],
                             hraw[:, ic * 512:(ic + 1) * 512], start=(ic == 0),
                             stop=(ic == 31)),
                        r=[wd_, (hraw, ic)], w=[psd])
                VOP(call('tensor_tensor', tq[:, 0:512], psd[:, 0:512], sxib[:, 0:512],
                         ALU.mult),
                    r=[psd, sxib], w=[tq])
                SOP(call('activation', gs_[:, 0:512], tq[:, 0:512], AF.Silu,
                         bias=0.0, scale=swcol[:, 20:21]),
                    r=[tq, (swcol, ('halfm', 2))], w=[gs_])
                VOP(call('tensor_tensor', tu[:, 0:512],
                         s_sb[:, oc2 * T + tt * SP: oc2 * T + tt * SP + SP],
                         gs_[:, 0:512], ALU.mult),
                    r=[(s_sb, (oc2, tt)), gs_], w=[tu])
                VOP(call('tensor_tensor',
                         x_sb[:, oc2 * T + tt * SP: oc2 * T + tt * SP + SP],
                         x_sb[:, oc2 * T + tt * SP: oc2 * T + tt * SP + SP],
                         tu[:, 0:512], ALU.add),
                    r=[tu, (x_sb, oc2)], w=[(x_sb, oc2)])

    def emit_l1a(xn1, bb):
        qoc, koc, voc, qk, cc = bb['qoc'], bb['koc'], bb['voc'], bb['qk'], bb['cc']
        embc, pmbc = bb['embc'], bb['pmbc']
        mch, smch, lnd = bb['mch'], bb['smch'], bb['lnd']
        emv, pmv = bb['emv'], bb['pmv']
        onesT, iota = bb['onesT'], bb['iota']
        wsrc = scr['qkvB'].rearrange("o p i c -> o p (i c)")
        GOP(call('iota', iota[:, :], [[1, T]], base=0, channel_multiplier=0, allow_small_or_imprecise_dtypes=True), w=[iota])
        GOP(call('memset', onesT[:, :], 1.0), w=[onesT])
        for g in range(NCH):
            for dst, ocw, key in ((qoc, g, 'q'), (koc, 8 + g, 'k'), (voc, 16 + g, 'v')):
                wb_ = wbuf[{'q': 0, 'k': 1, 'v': 2}[key]]
                DMA('sync', call('dma_start', wb_[:, :], wsrc[ocw]),
                    r=[(scr['qkvB'], ocw)], w=[wb_])
                for tt in range(4):
                    ps = PS[tt % 2]
                    for ic in range(NCH):
                        TOP(call('matmul', ps[:, 0:512],
                                 wb_[:, ic * 128:(ic + 1) * 128],
                                 xn1[:, ic * T + tt * SP: ic * T + tt * SP + SP],
                                 start=(ic == 0), stop=(ic == NCH - 1)),
                            r=[(xn1, ic), wb_], w=[ps])
                    VOP(call('tensor_copy', dst[:, tt * SP:(tt + 1) * SP], ps[:, 0:512]),
                        r=[ps], w=[(dst, tt)])
            VOP(call('tensor_mul', qk[:, :], qoc[:, :], koc[:, :]),
                r=[(qoc, tt) for tt in range(4)] + [(koc, tt) for tt in range(4)],
                w=[(qk, tt) for tt in range(4)])
            for tt in range(4):
                TOP(call('matmul', PS[2][:, 0:512], consts["ind2e"][:, :],
                         qk[:, tt * SP:(tt + 1) * SP], start=True, stop=True),
                    r=[(qk, tt), consts["ind2e"]], w=[PS[2]])
                VOP(call('tensor_scalar', mch[:, 0:512], PS[2][:, 0:512], 0.0, None,
                         ALU.max),
                    r=[PS[2]], w=[mch])
                VOP(call('tensor_scalar', smch[:, 0:512], PS[2][:, 0:512], 0.0, None,
                         ALU.min),
                    r=[PS[2]], w=[smch])
                SOP(call('activation', emv[:, 0:512], mch[:, 0:512], AF.Exp,
                         bias=0.0, scale=capv(-1.0)),
                    r=[mch], w=[emv])
                SOP(call('activation', pmv[:, 0:512], smch[:, 0:512], AF.Exp,
                         bias=0.0, scale=1.0),
                    r=[smch], w=[pmv])
                VOP(call('tensor_tensor', lnd[:, 0:512],
                         iota[:, tt * SP:(tt + 1) * SP], emv[:, 0:512], ALU.mult),
                    r=[emv, iota], w=[lnd])
                VOP(call('tensor_tensor', lnd[:, 0:512], lnd[:, 0:512], pmv[:, 0:512],
                         ALU.add),
                    r=[lnd, pmv], w=[lnd])
                SOP(call('activation', lnd[:, 0:512], lnd[:, 0:512], AF.Ln,
                         bias=0.0, scale=1.0),
                    r=[lnd], w=[lnd])
                VOP(call('tensor_tensor', mch[:, 0:512], mch[:, 0:512], lnd[:, 0:512],
                         ALU.add),
                    r=[mch, lnd], w=[mch])
                SOP(call('activation', embc[:, tt * SP:(tt + 1) * SP], mch[:, 0:512],
                         AF.Exp, bias=0.0, scale=capv(-1.0)),
                    r=[mch], w=[(embc, tt)])
                VOP(call('tensor_tensor', smch[:, 0:512], smch[:, 0:512], lnd[:, 0:512],
                         ALU.subtract),
                    r=[smch, lnd], w=[smch])
                SOP(call('activation', pmbc[:, tt * SP:(tt + 1) * SP], smch[:, 0:512],
                         AF.Exp, bias=0.0, scale=1.0),
                    r=[smch], w=[(pmbc, tt)])
            GOP(call('memset', cc[:, 0:1], 0.0), w=[(cc, 'z')])
            VOP(call('tensor_tensor_scan', cc[:, 1:T], onesT[:, 0:T - 1],
                     voc[:, 0:T - 1], 0.0, ALU.mult, ALU.add),
                r=[(voc, tt) for tt in range(4)] + [onesT],
                w=[(cc, tt) for tt in range(4)])
            for tt in range(4):
                VOP(call('tensor_tensor', qk[:, tt * SP:(tt + 1) * SP],
                         cc[:, tt * SP:(tt + 1) * SP],
                         embc[:, tt * SP:(tt + 1) * SP], ALU.mult),
                    r=[(cc, tt), (cc, 'z'), (embc, tt)], w=[(qk, tt)])
                VOP(call('tensor_tensor', cc[:, tt * SP:(tt + 1) * SP],
                         voc[:, tt * SP:(tt + 1) * SP],
                         pmbc[:, tt * SP:(tt + 1) * SP], ALU.mult),
                    r=[(voc, tt), (pmbc, tt), (cc, tt)], w=[(cc, tt)])
                VOP(call('tensor_tensor',
                         s_sb[:, g * T + tt * SP: g * T + tt * SP + SP],
                         qk[:, tt * SP:(tt + 1) * SP], cc[:, tt * SP:(tt + 1) * SP],
                         ALU.add),
                    r=[(qk, tt), (cc, tt)], w=[(s_sb, (g, tt))])

    def emit_l1b(xn1, bb):
        wbufD = bb['wbufD']
        hg, tq, tu, gs_ = bb['hg'], bb['tq'], bb['tu'], bb['gs']
        b1sb, b2sb = bb['b1sb'], bb['b2sb']
        w1v = scr['w1q'].rearrange("o p i c -> o p (i c)")
        w2v = scr['w2q'].rearrange("o p i c -> o p (i c)")
        DMA('sync', call('dma_start', b1sb[:, :], din['b1']), w=[b1sb])
        DMA('sync', call('dma_start', b2sb[:, :], din['b2']), w=[b2sb])
        for tt in range(4):
            for oc in range(32):
                par = oc % 2
                w1_ = wbuf[par]
                DMA('sync', call('dma_start', w1_[:, :], w1v[oc]), w=[w1_])
                ps = PS[par]
                for ic in range(NCH):
                    TOP(call('matmul', ps[:, 0:512], w1_[:, ic * 128:(ic + 1) * 128],
                             xn1[:, ic * T + tt * SP: ic * T + tt * SP + SP],
                             start=(ic == 0), stop=(ic == NCH - 1)),
                        r=[(xn1, ic), w1_], w=[ps])
                SOP(call('activation', hg[:, oc * 512:(oc + 1) * 512], ps[:, 0:512],
                         AF.Gelu, bias=b1sb[:, oc:oc + 1], scale=capv(0.5)),
                    r=[ps, b1sb, (cpool, CONST_VALUES.index(0.5))], w=[(hg, oc)])
            for oc2 in range(NCH):
                par = oc2 % 2
                w2_ = wbufD[par]
                DMA('sync', call('dma_start', w2_[:, :], w2v[oc2]), w=[w2_])
                psd = PS[4 + par]
                for ic in range(32):
                    TOP(call('matmul', psd[:, 0:512], w2_[:, ic * 128:(ic + 1) * 128],
                             hg[:, ic * 512:(ic + 1) * 512], start=(ic == 0),
                             stop=(ic == 31)),
                        r=[w2_, (hg, ic)], w=[psd])
                SOP(call('activation', gs_[:, 0:512], psd[:, 0:512], AF.Silu,
                         bias=b2sb[:, oc2:oc2 + 1], scale=capv(0.5)),
                    r=[psd, b2sb], w=[gs_])
                VOP(call('tensor_tensor', tu[:, 0:512],
                         s_sb[:, oc2 * T + tt * SP: oc2 * T + tt * SP + SP],
                         gs_[:, 0:512], ALU.mult),
                    r=[(s_sb, (oc2, tt)), gs_], w=[tu])
                VOP(call('tensor_tensor',
                         x_sb[:, oc2 * T + tt * SP: oc2 * T + tt * SP + SP],
                         x_sb[:, oc2 * T + tt * SP: oc2 * T + tt * SP + SP],
                         tu[:, 0:512], ALU.add),
                    r=[tu, (x_sb, oc2)], w=[(x_sb, oc2)])
                DMA('sync', call('dma_start', x_out[oc2][:, tt * SP:(tt + 1) * SP],
                                 x_sb[:, oc2 * T + tt * SP: oc2 * T + tt * SP + SP]),
                    r=[(x_sb, oc2)], w=[(x_out, (oc2, tt))])

    from contextlib import ExitStack as ES

    with ES() as ph:
        Eh = ph.enter_context
        wstg = [Eh(SBT("wstg%d" % i, [128, 2048], F32)) for i in range(2)]
        wsg = [Eh(SBT("wsg%d" % i, [128, 2048], BF16)) for i in range(2)]
        xbf = Eh(SBT("xbf", [128, NCH * T], BF16))
        x2s = [Eh(SBT("x2s%d" % i, [128, 1024], BF16)) for i in range(1)]
        pF = Eh(SBT("pF", [128, 544], F32))
        lnrow_box[0] = Eh(SBT("lnrow0", [1, 1024], F32))
        xn0 = xbf

        P.paranoid = P4_PRE_PAR or GLOBAL_PARANOID
        if PHASES >= 1:
            emit_stats(QSPEC_L0, wstg)
        if PHASES >= 2:
            emit_quant(QSPEC_L0, wstg, wsg)
        if PHASES >= 3:
            emit_castpass(wstg, wsg)
        if PHASES >= 4:
            for ic in range(NCH):
                if ic % 2 == 0:
                    VOP(call('tensor_copy', xbf[:, ic * T:(ic + 1) * T],
                             x_sb[:, ic * T:(ic + 1) * T]),
                        r=[(x_sb, ic)], w=[(xbf, ic)])
                else:
                    SOP(call('copy', xbf[:, ic * T:(ic + 1) * T],
                             x_sb[:, ic * T:(ic + 1) * T]),
                        r=[(x_sb, ic)], w=[(xbf, ic)])
        if PHASES >= 4:
            P.paranoid = P4_LN_PAR or GLOBAL_PARANOID
            emit_ln(xbf, x2s, xn0, pF, xn_dram=[scr['xn_d'][ic] for ic in range(NCH)])
        P.paranoid = GLOBAL_PARANOID
        P.barrier()
        if PHASES == 4 and DUMP4:
            dstg = wstg[0]
            DMA('sync', call('dma_start', x_out[0][:, 0:544], pF[:, :]),
                r=[(pF, k) for k in ('a', 'v', 'r', 'p', 'sub')],
                w=[(x_out, ('f', 0))])
            DMA('sync', call('dma_start', x_out[1][0:32, 0:128], lnrows32[:, :]),
                r=[lnrows32], w=[(x_out, ('f', 1))])
            VOP(call('tensor_copy', dstg[:, 0:32], PS[0][:, 0:32]),
                r=[PS[0]], w=[(dstg, 'd')])
            VOP(call('tensor_copy', dstg[:, 512:1024], xbf[:, 0:512]),
                r=[(xbf, 0)], w=[(dstg, 'd')])
            DMA('sync', call('dma_start', x_out[2], dstg[:, :]),
                r=[(dstg, 'd')], w=[(x_out, ('f', 2))])
            P.op('vector', call('engine_nop'),
                 reads=[(x_out, ('f', i)) for i in range(3)] +
                       [(x_out, k) for k in ('f3a', 'f3b', 'f3c')])

    with ES() as ph:
        Eh = ph.enter_context
        bb = {}
        bb['wbb'] = Eh(SBT("wbb", [128, 128], BF16))
        bb['bpre'] = Eh(SBT("bpre", [16, 512], F32))
        bb['bbf'] = Eh(SBT("bbf", [16, T], BF16))
        bb['nbbf'] = Eh(SBT("nbbf", [16, T], BF16))
        bb['bbf2'] = Eh(SBT("bbf2", [2, T], BF16))
        bb['nbbf2'] = Eh(SBT("nbbf2", [2, T], BF16))
        bb['cdbuf'] = [Eh(SBT("cdbuf%d" % i, [128, 512], BF16)) for i in range(2)]
        bb['raw'] = [Eh(SBT("raw%d" % i, [128, T + 8], BF16)) for i in range(2)]
        bb['sil'] = [Eh(SBT("sil%d" % i, [128, T], BF16)) for i in range(2)]
        bb['sq'] = [Eh(SBT("sqb%d" % i, [128, 512], BF16)) for i in range(1)]
        bb['lnb'] = [Eh(SBT("lnbb%d" % i, [128, 512], F32)) for i in range(1)]
        bb['rr'] = [Eh(SBT("rrb%d" % i, [128, 512], F32)) for i in range(2)]
        bb['ob1'] = [Eh(SBT("ob1%d" % i, [128, T], BF16)) for i in range(2)]
        bb['ob2'] = [Eh(SBT("ob2%d" % i, [128, T], BF16)) for i in range(2)]
        xn0b = Eh(SBT("xn0b", [128, NCH * T], BF16))
        if PHASES >= 5:
            for ic in range(NCH):
                DMA('sync', call('dma_start', xn0b[:, ic * T:(ic + 1) * T], scr['xn_d'][ic]),
                    r=[(scr['xn_d'], (ic, tt)) for tt in range(4)], w=[(xn0b, ic)])
            for i in range(2):
                GOP(call('memset', bb['raw'][i][:, 0:3], 0.0), w=[(bb['raw'][i], 'halo')])
        if PHASES >= 5:
            emit_a1(xn0b, bb)
        P.barrier()

    with ES() as ph:
        Eh = ph.enter_context
        bb = {}
        bb['spq'] = [Eh(SBT("spq%d" % i, [128, NCH * SP], BF16)) for i in range(2)]
        bb['spk'] = [Eh(SBT("spk%d" % i, [128, NCH * SP], BF16)) for i in range(2)]
        bb['spkbn'] = [Eh(SBT("spkbn%d" % i, [128, NCH * SP], BF16)) for i in range(2)]
        bb['spvb'] = [Eh(SBT("spvb%d" % i, [128, NCH * SP], BF16)) for i in range(2)]
        bb['pow'] = [Eh(SBT("pow%d" % i, [128, 1024], BF16)) for i in range(NLEV)]
        bb['Msb'] = [Eh(SBT("Msb%d" % i, [128, 1024], BF16)) for i in range(1)]
        bb['Dtsb'] = [Eh(SBT("Dtsb%d" % i, [128, 512], BF16)) for i in range(1)]
        bb['Xf'] = [Eh(SBT("Xf%d" % i, [128, 256], F32)) for i in range(2)]
        bb['Xb'] = [Eh(SBT("Xb%d" % i, [128, 256], BF16)) for i in range(2)]
        bb['vbtk'] = [Eh(SBT("vbtk%d" % i, [128, 512], BF16)) for i in range(1)]
        bb['otmn'] = Eh(SBT("otmn", [128, 256], BF16))
        bb['o2s'] = Eh(SBT("o2s", [128, 64], F32))
        bb['ssd'] = Eh(SBT("ssd", [128, 8], F32))
        bb['rno'] = Eh(SBT("rno", [128, 8], F32))
        bb['ofm'] = Eh(SBT("ofm", [128, NCH * SP], BF16))
        for nm, shape, dt in MASK_SPECS:
            consts[nm] = Eh(SBT("c_" + nm, list(shape), dt))
            DMA('sync', call('dma_start', consts[nm][:, :], din[nm]), w=[consts[nm]])
        if PHASES >= 6:
            if A2_PARANOID:
                P.paranoid = True
            emit_a2(bb)
            P.paranoid = False
        P.barrier()

    with ES() as ph:
        Eh = ph.enter_context
        bb = {}
        bb['xnt'] = [Eh(SBT("xnt0", [128, NCH * SP], BF16))]
        bb['xq'] = Eh(SBT("xq", [128, NCH * SP], BF16))
        bb['hraw'] = Eh(SBT("hraw", [128, 32 * 512], FP16))
        bb['wbufD'] = [Eh(SBT("wbufD%d" % i, [128, 4096], BF16)) for i in range(2)]
        bb['mpos'] = Eh(SBT("mpos", [128, 512], F32))
        bb['mneg'] = Eh(SBT("mneg", [128, 512], F32))
        bb['mx4'] = Eh(SBT("mx4", [128, 16], F32))
        bb['mrows'] = Eh(SBT("mrows", [8, 128], F32))
        bb['mrow1'] = Eh(SBT("mrow1", [1, 1024], F32))
        bb['sxb'] = Eh(SBT("sxb", [128, 512], F32))
        bb['sxib'] = Eh(SBT("sxib", [128, 512], F32))
        bb['tq'] = Eh(SBT("tqb", [128, 512], F32))
        bb['gs'] = Eh(SBT("gsb", [128, 512], F32))
        bb['tu'] = Eh(SBT("tub", [128, 512], F32))
        if PHASES >= 7:
            emit_l0b(bb)
        P.barrier()

    with ES() as ph:
        Eh = ph.enter_context
        xn1 = Eh(SBT("xn1", [128, NCH * T], BF16))
        with ES() as pl:
            El = pl.enter_context
            xbf1 = xn1
            x2s1 = [El(SBT("x2t%d" % i, [128, 1024], BF16)) for i in range(1)]
            pF1 = El(SBT("pF1", [128, 544], F32))
            lnrow_box[0] = El(SBT("lnrow1", [1, 1024], F32))
            wstg1 = [El(SBT("wstg1_%d" % i, [128, 2048], F32)) for i in range(1)]
            wsg1 = [El(SBT("wsg1_%d" % i, [128, 2048], BF16)) for i in range(2)]
            if PHASES >= 8:
                P.paranoid = L8_STATS_PAR
                emit_stats(QSPEC_L1, wstg1)
                emit_quant(QSPEC_L1, wstg1, wsg1)
            P.paranoid = L8_COPY_PAR
            for ic in range(NCH):
                if ic % 2 == 0:
                    VOP(call('tensor_copy', xbf1[:, ic * T:(ic + 1) * T],
                             x_sb[:, ic * T:(ic + 1) * T]),
                        r=[(x_sb, ic)], w=[(xbf1, ic)])
                else:
                    SOP(call('copy', xbf1[:, ic * T:(ic + 1) * T],
                             x_sb[:, ic * T:(ic + 1) * T]),
                        r=[(x_sb, ic)], w=[(xbf1, ic)])
            if PHASES >= 8:
                P.paranoid = L8_LN_PAR
                emit_ln(xbf1, x2s1, xn1, pF1)
            P.paranoid = False
            P.barrier()
            if PHASES == 8 and DUMP8 == 'lni':
                dstg = El(SBT("dbgstg", [128, T], F32))
                P.barrier()
                DMA('sync', call('dma_start', x_out[0][:, 0:544], pF1[:, :]),
                    r=[(pF1, k) for k in ('a', 'v', 'r', 'p', 'sub')],
                    w=[(x_out, ('f', 0))])
                DMA('sync', call('dma_start', x_out[1][0:32, 0:128], lnrows32[:, :]),
                    r=[lnrows32], w=[(x_out, ('f', 1))])
                DMA('sync', call('dma_start', x_out[2][0:1, 0:1024],
                                 lnrow_box[0][0:1, :]),
                    r=[(lnrow_box[0], 'm'), (lnrow_box[0], 'r')],
                    w=[(x_out, ('f', 2))])
                VOP(call('tensor_copy', dstg[:, 0:512], PS[2][:, 0:512]),
                    r=[PS[2]], w=[(dstg, 'd')])
                VOP(call('tensor_copy', dstg[:, 512:1024], PS[3][:, 0:512]),
                    r=[PS[3]], w=[(dstg, 'd')])
                VOP(call('tensor_copy', dstg[:, 1024:1056], PS[0][:, 0:32]),
                    r=[PS[0]], w=[(dstg, 'd')])
                DMA('sync', call('dma_start', x_out[3], dstg[:, :]),
                    r=[(dstg, 'd')], w=[(x_out, ('f', 3))])
                VOP(call('tensor_copy', dstg[:, :], xbf1[:, 0:T]),
                    r=[(xbf1, 0)], w=[(dstg, 'd')])
                DMA('sync', call('dma_start', x_out[4], dstg[:, :]),
                    r=[(dstg, 'd')], w=[(x_out, ('f', 4))])
                VOP(call('tensor_copy', dstg[:, :], xn1[:, 0:T]),
                    r=[(xn1, 0)], w=[(dstg, 'd')])
                DMA('sync', call('dma_start', x_out[5], dstg[:, :]),
                    r=[(dstg, 'd')], w=[(x_out, ('f', 5))])
                DMA('sync', call('dma_start', x_out[6], x_sb[:, 0:T]),
                    r=[(x_sb, 0)], w=[(x_out, ('f', 6))])
                VOP(call('tensor_copy', dstg[:, :], xn1[:, 7 * T:8 * T]),
                    r=[(xn1, 7)], w=[(dstg, 'd')])
                DMA('sync', call('dma_start', x_out[7], dstg[:, :]),
                    r=[(dstg, 'd')], w=[(x_out, ('f', 7))])
            elif PHASES == 8:
                dstg = El(SBT("dbgstg", [128, T], F32))
                dsrc = {'xn1': xn1, 'xbf1': xbf1}.get(DUMP8)
                for ic in range(NCH):
                    if dsrc is None:
                        DMA('sync', call('dma_start', x_out[ic],
                                         x_sb[:, ic * T:(ic + 1) * T]),
                            r=[(x_sb, ic)], w=[(x_out, ('f', ic))])
                        continue
                    VOP(call('tensor_copy', dstg[:, :], dsrc[:, ic * T:(ic + 1) * T]),
                        r=[(dsrc, ic)], w=[(dstg, 'd')])
                    DMA('sync', call('dma_start', x_out[ic], dstg[:, :]),
                        r=[(dstg, 'd')], w=[(x_out, ('f', ic))])

        with ES() as pa:
            Ea = pa.enter_context
            bb = {}
            bb['qoc'] = Ea(SBT("qoc", [128, T], BF16))
            bb['koc'] = Ea(SBT("koc", [128, T], BF16))
            bb['voc'] = Ea(SBT("voc", [128, T], BF16))
            bb['qk'] = Ea(SBT("qkb", [128, T], BF16))
            bb['cc'] = Ea(SBT("ccb", [128, T], F32))
            bb['embc'] = Ea(SBT("embc", [128, T], BF16))
            bb['pmbc'] = Ea(SBT("pmbc", [128, T], BF16))
            bb['mch'] = Ea(SBT("mch", [128, 512], F32))
            bb['smch'] = Ea(SBT("smch", [128, 512], F32))
            bb['lnd'] = Ea(SBT("lnd", [128, 512], F32))
            bb['emv'] = Ea(SBT("emv", [128, 512], F32))
            bb['pmv'] = Ea(SBT("pmv", [128, 512], F32))
            bb['onesT'] = Ea(SBT("onesT", [128, T], BF16))
            bb['iota'] = Ea(SBT("iotab", [128, T], F32))
            if PHASES >= 9:
                emit_l1a(xn1, bb)
            P.barrier()

        with ES() as pa:
            Ea = pa.enter_context
            bb = {}
            bb['hg'] = Ea(SBT("hgb", [128, 32 * 512], BF16))
            bb['tq'] = Ea(SBT("tqc", [128, 512], F32))
            bb['tu'] = Ea(SBT("tuc", [128, 512], F32))
            bb['gs'] = Ea(SBT("gsc", [128, 512], F32))
            bb['wbufD'] = [Ea(SBT("wbufE%d" % i, [128, 4096], BF16)) for i in range(2)]
            bb['b1sb'] = Ea(SBT("b1sb", [128, 32], F32))
            bb['b2sb'] = Ea(SBT("b2sb", [128, 8], F32))
            if PHASES >= 10:
                emit_l1b(xn1, bb)

    if PHASES >= 10:
        P.op('vector', call('engine_nop'),
             reads=[(x_out, (oc2, tt)) for oc2 in range(NCH) for tt in range(4)])
    else:
        if PHASES == 8 or (PHASES == 4 and DUMP4):
            pass
        elif PHASES == 9:
            dstg = stack.enter_context(SBT("dbgstg9", [128, T], F32))
            for ic in range(NCH):
                VOP(call('tensor_copy', dstg[:, :], s_sb[:, ic * T:(ic + 1) * T]),
                    r=[(s_sb, (ic, tt)) for tt in range(4)], w=[(dstg, 'd')])
                DMA('sync', call('dma_start', x_out[ic], dstg[:, :]),
                    r=[(dstg, 'd')], w=[(x_out, ('f', ic))])
        else:
            for ic in range(NCH):
                DMA('sync', call('dma_start', x_out[ic], x_sb[:, ic * T:(ic + 1) * T]),
                    r=[(x_sb, ic)], w=[(x_out, ('f', ic))])
        P.op('vector', call('engine_nop'),
             reads=[(x_out, ('f', ic)) for ic in range(NCH)])

    if DBG_SCR:
        dstg2 = stack.enter_context(SBT("dbgstg2", [128, T], F32))
        for ic in range(NCH):
            VOP(call('tensor_copy', dstg2[:, :], s_sb[:, ic * T:(ic + 1) * T]),
                r=[(s_sb, (ic, tt)) for tt in range(4)], w=[(dstg2, 'd')])
            DMA('sync', call('dma_start', s_out[ic], dstg2[:, :]),
                r=[(dstg2, 'd')], w=[(s_out, ('s', ic))])
        P.op('vector', call('engine_nop'),
             reads=[(s_out, ('s', ic)) for ic in range(NCH)])

    P.finalize()
    stack.close()
    return nc


_NC_CACHE = [None]


def kernel(**inputs):
    inputs = {k: np.asarray(v) for k, v in inputs.items()}
    H = _host_prep(inputs)
    x = inputs['L0_attn_x']
    if _NC_CACHE[0] is None:
        _NC_CACHE[0] = build_program()
    nc = _NC_CACHE[0]
    in_maps = []
    for b in range(8):
        m = {"x_in": np.ascontiguousarray(x[b].T).reshape(8, 128, T)}
        for nm, _ in WEIGHT_SPECS:
            m[nm] = H[nm]
        for nm, _, _ in CONST_SPECS + MASK_SPECS:
            m[nm] = H[nm]
        in_maps.append(m)
    res = run_bass_kernel_spmd(nc, in_maps, core_ids=list(range(8)))
    outs = []
    for r in res.results:
        o = np.asarray(r["x_out"]).reshape(1024, T)
        outs.append(np.ascontiguousarray(o.T))
    return np.stack(outs)


# version 04 (final)
# speedup vs baseline: 1.0426x; 1.0426x over previous
import numpy as np
import ml_dtypes
import concourse.bass as bass
import concourse.mybir as mybir
from concourse.bass_utils import run_bass_kernel_spmd

F32 = mybir.dt.float32
BF16 = mybir.dt.bfloat16
FP16 = mybir.dt.float16
I8 = mybir.dt.int8
AF = mybir.ActivationFunctionType
ALU = mybir.AluOpType
AX = mybir.AxisListType
BF = ml_dtypes.bfloat16

T = 2048
D = 1024
NCH = 8
SP = 512
NSP = T // SP
C = 64
NLEV = 5
CRND = 12582912.0
LN8 = float(np.log(0.125))

ENGS = ('tensor', 'vector', 'scalar', 'sync', 'gpsimd')
CONST_VALUES = [1.0, -1.0, 0.5, -0.5, 1e-30, 1e-5, 1e-6, LN8, -1e-5, 1e-8, -1e-8]


def call(method, *args, **kw):
    def fn(e):
        return getattr(e, method)(*args, **kw)
    fn.method = method
    fn.args = args
    fn.kw = kw
    import sys
    f = sys._getframe(1)
    fn.site = "%s:%d" % (f.f_code.co_filename.rsplit('/', 1)[-1], f.f_lineno)
    return fn


def _is_ap(o):
    return hasattr(o, 'tensor') and hasattr(o, 'ap') and hasattr(o, 'offset')


def _ap_elems(a):
    idx = np.array([int(a.offset)], dtype=np.int64)
    for st_, n in a.ap:
        st_, n = int(st_), int(n)
        idx = (idx[:, None] + (np.arange(n, dtype=np.int64) * st_)[None, :]).reshape(-1)
    return a.tensor, idx


class APCheck:
    MAXEL = 3 << 20

    def __init__(self):
        self.kl = list(ENGS) + ['dma:' + e for e in ENGS]
        self.ki = {k: i for i, k in enumerate(self.kl)}
        self.nk = len(self.kl)
        self.vecs = {k: [] for k in self.kl}
        self.lw = {}
        self.rd = {}
        self.meta = {}
        self.opinfo = [None]
        self.skipped = set()
        self.pre = set()
        self.viol = {}

    def _tarr(self, t):
        tid = id(t)
        if tid in self.lw:
            return tid
        if tid in self.skipped:
            return None
        n = 1
        for d in t.shape:
            n *= int(d)
        if n > self.MAXEL:
            self.skipped.add(tid)
            return None
        self.lw[tid] = np.zeros(n, np.int32)
        self.rd[tid] = {}
        self.meta[tid] = (t.name, n)
        return tid

    def mark_pre(self, t):
        self.pre.add(id(t))

    def _report(self, kind, site, m, t, wsite, nel):
        key = (kind, site, m, t.name, wsite)
        c = self.viol.get(key, 0)
        self.viol[key] = c + nel
        if c == 0:
            print("!! AP-%s: %s %s on %s (writer/reader at %s)"
                  % (kind, site, m, t.name, wsite))

    def op(self, eng, idx, dma, fn, waits):
        ei = self.ki[eng]
        prev = self.vecs[eng][idx - 1] if idx > 0 else np.zeros(self.nk, np.int32)
        v = prev.copy()
        for (k, i) in waits:
            base = k[4:] if k.startswith('dma:') else k
            wv = self.vecs[base][i]
            np.maximum(v, wv, out=v)
            if v[self.ki[k]] < i:
                v[self.ki[k]] = i
        v[ei] = idx
        self.vecs[eng].append(v)
        ckey = ('dma:' + eng) if dma else eng
        cki = self.ki[ckey]
        m = getattr(fn, 'method', None)
        if m is None:
            return
        args, kw = fn.args, fn.kw
        wr, rd = [], []
        aps = [a for a in args if _is_ap(a)]
        if m == 'matmul' and not kw.get('start', False) and aps:
            rd.append(aps[0])
        if aps:
            wr.append(aps[0])
            rd.extend(aps[1:])
        for kname, kv in kw.items():
            if _is_ap(kv):
                (wr if kname == 'accum_out' else rd).append(kv)
        opnum = len(self.opinfo)
        self.opinfo.append((ckey, idx, getattr(fn, 'site', '?'), m))
        for a in rd:
            t, ix = _ap_elems(a)
            tid = self._tarr(t)
            if tid is None or tid in self.pre:
                continue
            lw = self.lw[tid]
            ws = np.unique(lw[ix])
            for w in ws:
                if w == 0:
                    self._report('UBD', self.opinfo[opnum][2], m, t, '-',
                                 int((lw[ix] == 0).sum()))
                    continue
                wk, wi, wsite, wm = self.opinfo[w]
                if wk != ckey and v[self.ki[wk]] < wi:
                    self._report('RAW', self.opinfo[opnum][2], m, t, wsite,
                                 int((lw[ix] == w).sum()))
            ra = self.rd[tid].setdefault(cki, np.zeros(self.meta[tid][1], np.int32))
            ra[ix] = opnum
        for a in wr:
            t, ix = _ap_elems(a)
            tid = self._tarr(t)
            if tid is None:
                continue
            lw = self.lw[tid]
            ws = np.unique(lw[ix])
            for w in ws:
                if w == 0:
                    continue
                wk, wi, wsite, wm = self.opinfo[w]
                if wk != ckey and v[self.ki[wk]] < wi:
                    self._report('WAW', self.opinfo[opnum][2], m, t, wsite,
                                 int((lw[ix] == w).sum()))
            for kk, ra in self.rd[tid].items():
                if kk == cki:
                    continue
                rs = np.unique(ra[ix])
                for r in rs:
                    if r == 0:
                        continue
                    rk, rdi, rsite, rm = self.opinfo[r]
                    if v[self.ki[rk]] < rdi:
                        self._report('WAR', self.opinfo[opnum][2], m, t, rsite,
                                     int((ra[ix] == r).sum()))
            lw[ix] = opnum
        return

    def summary(self):
        print("== APCheck: %d distinct violation sites" % len(self.viol))
        for k, n in sorted(self.viol.items(), key=lambda x: -x[1])[:60]:
            print("  %s x%d: op@%s %s on %s peer@%s" % (k[0], n, k[1], k[2], k[3], k[4]))


class Prog:
    def __init__(self, nc):
        self.nc = nc
        self.paranoid = False
        self.ops = {e: [] for e in ENGS}
        self.marked = {e: set() for e in ENGS}
        self.tiles = {}
        self.barrier_deps = []
        self.sems = {}
        self.gseq = 0
        self.live = {}
        self.freed = {}
        self.viol = 0
        self.apc = APCheck() if APCHECK else None
        self.haz_recent = {'vector': [], 'scalar': []}
        self.haz_junk = {}
        self.nspacers = 0

    HAZ_WIN = 2
    HAZ_SIZE = 256

    @staticmethod
    def _ap_bounds(a):
        lo = int(a.offset)
        hi = lo
        nfree = 1
        dims = list(a.ap)
        for di, (st_, num) in enumerate(dims):
            st_, num = int(st_), int(num)
            if di > 0 or len(dims) == 1:
                nfree *= num
            if st_ >= 0:
                hi += (num - 1) * st_
            else:
                lo += (num - 1) * st_
        return id(a.tensor), lo, hi + 1, nfree

    def _haz_guard(self, eng, fn):
        if eng not in self.haz_recent or eng not in self.haz_junk:
            return
        m = getattr(fn, 'method', None)
        if m is None:
            return
        aps = [a for a in fn.args if _is_ap(a)] + \
              [v for v in fn.kw.values() if _is_ap(v)]
        if not aps:
            return
        reads = aps[1:] if len(aps) > 1 else []
        if m == 'matmul' and not fn.kw.get('start', False):
            reads = aps
        rb = [self._ap_bounds(a) for a in reads]
        need = 0
        for dist, wlist in enumerate(reversed(self.haz_recent[eng])):
            for (tid, lo, hi, n) in wlist:
                for (rt, rl, rh, rn) in rb:
                    if rt == tid and rl < hi and lo < rh:
                        need = max(need, self.HAZ_WIN - dist)
        for _ in range(need):
            self.nspacers += 1
            jk = self.haz_junk[eng]
            if eng == 'vector':
                f = call('tensor_copy', jk[:, 0:512], jk[:, 512:1024])
            else:
                f = call('copy', jk[:, 0:512], jk[:, 512:1024])
            self._emit_plain(eng, f)

    def _emit_plain(self, eng, fn):
        lst = self.ops[eng]
        idx = len(lst)
        if self.apc is not None:
            self.apc.op(eng, idx, False, fn, [])
        lst.append({'waits': [], 'fn': fn, 'dma': False, 'idx': idx})
        self.haz_recent[eng].append([])
        del self.haz_recent[eng][:-self.HAZ_WIN]

    def _haz_record(self, eng, fn):
        if eng not in self.haz_recent:
            return
        ws = []
        m = getattr(fn, 'method', None)
        if m is not None:
            aps = [a for a in fn.args if _is_ap(a)]
            wr = aps[:1] + [v for k, v in fn.kw.items()
                            if k == 'accum_out' and _is_ap(v)]
            for a in wr:
                b = self._ap_bounds(a)
                if b[3] <= self.HAZ_SIZE:
                    ws.append(b)
        self.haz_recent[eng].append(ws)
        del self.haz_recent[eng][:-self.HAZ_WIN]

    def _alloc(self, h):
        import concourse.mybir as _mb
        per = 1
        for d in h.shape[1:]:
            per *= int(d)
        per *= _mb.dt.size(h.dtype)
        addr = self.nc.lookup_mloc(h).addr
        rec = {'name': h.name, 'addr': addr, 'size': per, 'h': h,
               'a_seq': self.gseq}
        for r2 in self.live.values():
            if addr < r2['addr'] + r2['size'] and r2['addr'] < addr + per:
                self.viol += 1
                print("!! ALLOC-OVERLAP:", h.name, "(%d+%d)" % (addr, per),
                      "vs", r2['name'], "(%d+%d)" % (r2['addr'], r2['size']))
        self.live[id(h)] = rec

    def _free(self, h):
        rec = self.live.pop(id(h), None)
        if rec is not None:
            rec['f_seq'] = self.gseq
            self.freed[id(h)] = rec

    def _check_keys(self, eng, keys, mode):
        for t in keys:
            o = t[0] if isinstance(t, tuple) else t
            if id(o) in self.freed:
                self.viol += 1
                print("!! ZOMBIE-%s: op %s seq %d touches freed tensor %s"
                      % (mode, eng, self.gseq, self.freed[id(o)]['name']))

    @staticmethod
    def _tkey(t):
        if isinstance(t, tuple):
            return (id(t[0]), t[1])
        return (id(t), None)

    def _tile(self, t):
        k = self._tkey(t)
        if k not in self.tiles:
            self.tiles[k] = {'w': None, 'r': {}}
        return self.tiles[k]

    def op(self, eng, fn, reads=(), writes=(), dma=False):
        if not dma:
            self._haz_guard(eng, fn)
        if self.paranoid:
            self.barrier()
        self.gseq += 1
        self._check_keys(eng, reads, 'R')
        self._check_keys(eng, writes, 'W')
        lst = self.ops[eng]
        idx = len(lst)
        me = ('dma:' + eng) if dma else eng
        waits = []

        def need(dep):
            if dep is None:
                return
            k, i = dep
            if k == eng:
                return
            waits.append((k, i))

        for t in reads:
            rec = self._tile(t)
            if rec['w'] is None:
                o = t[0] if isinstance(t, tuple) else t
                nm = getattr(o, 'name', type(o).__name__)
                tg = t[1] if isinstance(t, tuple) else None
                self.viol += 1
                print("!! USE-BEFORE-DEF: %s reads (%s, %r) at seq %d"
                      % (eng, nm, tg, self.gseq))
            need(rec['w'])
            rec['r'][me] = idx
        for t in writes:
            rec = self._tile(t)
            need(rec['w'])
            for rk, ri in list(rec['r'].items()):
                if rk == eng:
                    continue
                waits.append((rk, ri))
            rec['w'] = (me, idx)
            rec['r'] = {}
        waits.extend(self.barrier_deps)
        if self.apc is not None:
            self.apc.op(eng, idx, dma, fn, waits)
        lst.append({'waits': waits, 'fn': fn, 'dma': dma, 'idx': idx})
        if dma:
            if eng in self.haz_recent:
                self.haz_recent[eng].append([])
                del self.haz_recent[eng][:-self.HAZ_WIN]
        else:
            self._haz_record(eng, fn)
        for (k, i) in waits:
            if not k.startswith('dma:'):
                self.marked[k].add(i)
        return (me, idx)

    def barrier(self):
        deps = []
        for e in ENGS:
            if not self.ops[e]:
                continue
            last = len(self.ops[e]) - 1
            ent = self.ops[e][last]
            if ent['dma']:
                deps.append(('dma:' + e, last))
            else:
                deps.append((e, last))
                self.marked[e].add(last)
            for j in range(last, -1, -1):
                if self.ops[e][j]['dma']:
                    if not ent['dma']:
                        deps.append(('dma:' + e, j))
                    break
        self.barrier_deps = deps

    def finalize(self):
        nc = self.nc
        rank = {}
        for e in ENGS:
            ms = sorted(self.marked[e])
            rank[e] = {i: r + 1 for r, i in enumerate(ms)}
        dmarank = {e: {} for e in ENGS}
        for e in ENGS:
            cnt = 0
            for j, ent in enumerate(self.ops[e]):
                if ent['dma']:
                    cnt += 1
                    dmarank[e][j] = cnt * 16

        from contextlib import ExitStack
        with ExitStack() as st:
            for e in ENGS:
                self.sems[e] = st.enter_context(nc.semaphore("sem_" + e))
                self.sems['dma:' + e] = st.enter_context(nc.semaphore("dsem_" + e))
            block = st.enter_context(nc.Block())

            def make_body(ename):
                ops = self.ops[ename]
                mset = self.marked[ename]
                dsem = self.sems['dma:' + ename]
                esem = self.sems[ename]
                sems = self.sems
                rk = rank
                dr = dmarank

                def body(eng):
                    lastw = {}
                    for ent in ops:
                        for (k, i) in ent['waits']:
                            if k.startswith('dma:'):
                                v = dr[k[4:]][i]
                            else:
                                v = rk[k][i]
                            if lastw.get(k, 0) >= v:
                                continue
                            eng.wait_ge(sems[k], v)
                            lastw[k] = v
                        if ent['dma'] and SERIALIZE_DMA:
                            v = dr[ename][ent['idx']] - 16
                            k = 'dma:' + ename
                            if v > 0 and lastw.get(k, 0) < v:
                                eng.wait_ge(dsem, v)
                                lastw[k] = v
                        inst = ent['fn'](eng)
                        if ent['dma']:
                            inst.then_inc(dsem, 16)
                        elif ent['idx'] in mset:
                            inst.then_inc(esem, 1)
                return body

            for ename in ENGS:
                getattr(block, ename)(make_body(ename))


def _tile_lhsT(WT, n_ic, n_oc):
    return np.ascontiguousarray(
        WT.reshape(n_ic, 128, n_oc, 128).transpose(2, 1, 0, 3))


def _conv_diag(conv):
    out = np.zeros((8, 128, 4, 128), dtype=np.float32)
    for oc in range(8):
        for p in range(128):
            out[oc, p, :, p] = conv[oc * 128 + p, :]
    return out


def _host_prep(inputs):
    H = {}
    f = np.ascontiguousarray
    H['wq'] = _tile_lhsT(f(inputs['L0_attn_Wq'].T), 8, 8)
    H['wk'] = _tile_lhsT(f(inputs['L0_attn_Wk'].T), 8, 8)
    H['wv'] = _tile_lhsT(f(inputs['L0_attn_Wv'].T), 8, 8)
    H['wo'] = _tile_lhsT(f(inputs['L0_attn_Wo'].T), 8, 8)
    H['wb'] = f(inputs['L0_attn_Wb'].T.reshape(8, 128, 16).transpose(1, 0, 2))
    H['cdq'] = _conv_diag(inputs['L0_attn_conv_q'])
    H['cdk'] = _conv_diag(inputs['L0_attn_conv_k'])
    H['cdv'] = _conv_diag(inputs['L0_attn_conv_v'])
    H['onw'] = f(np.broadcast_to(inputs['L0_attn_o_norm_w'], (128, 64))).astype(np.float32)
    H['wgate'] = _tile_lhsT(f(inputs['L0_ffn_w_gate'].T), 8, 32)
    H['wup'] = _tile_lhsT(f(inputs['L0_ffn_w_up'].T), 8, 32)
    H['wdown'] = _tile_lhsT(f(inputs['L0_ffn_w_down'].T), 32, 8)
    wqkv = inputs['L1_attn_wqkv']
    perm = []
    for part in range(3):
        for h in range(16):
            base = h * 192 + part * 64
            perm.extend(range(base, base + 64))
    H['wqkv'] = _tile_lhsT(f(wqkv[:, perm]), 8, 24)
    H['w1'] = _tile_lhsT(f(inputs['L1_ffn_W1'].T), 8, 32)
    H['w2'] = _tile_lhsT(f(inputs['L1_ffn_W2'].T), 32, 8)
    H['b1'] = f(inputs['L1_ffn_b1'].reshape(32, 128).T)
    H['b2'] = f(inputs['L1_ffn_b2'].reshape(8, 128).T)
    sl = np.tril(np.ones((C, C), dtype=np.float32), -1)
    z = np.zeros((C, C), dtype=np.float32)
    msl = np.block([[sl, z], [z, sl]])
    mm = np.concatenate([msl, msl.T], axis=1)
    H['mmask2'] = f(np.concatenate([mm, mm], axis=1)).astype(BF)
    tiu = np.triu(np.ones((C, C), dtype=np.float32), 0)
    tb = np.block([[tiu, z], [z, tiu]])
    H['tiumask4'] = f(np.concatenate([tb] * 4, axis=1)).astype(BF)
    H['ident'] = np.eye(128, dtype=BF)
    H['identf'] = np.eye(128, dtype=np.float32)
    bd = np.zeros((128, 128), dtype=np.float32)
    bd[:64, :64] = 1.0
    bd[64:, 64:] = 1.0
    H['ind2'] = bd.astype(BF)
    H['ind2e'] = (bd * 0.125).astype(BF)
    b2w = np.zeros((2, 128), dtype=np.float32)
    b2w[0, :64] = 1.0
    b2w[1, 64:] = 1.0
    H['bc2'] = b2w.astype(BF)
    H['onescol'] = np.ones((128, 1), dtype=BF)
    H['onescol_f'] = np.ones((128, 2), dtype=np.float32)
    H['half2'] = np.full((2, 128), 0.5, dtype=np.float32)
    H['onesrow_f'] = np.ones((1, 128), dtype=np.float32)
    return H


MASK_SPECS = [("mmask2", (128, 512), BF16), ("tiumask4", (128, 512), BF16)]
CONST_SPECS = [("ident", (128, 128), BF16), ("identf", (128, 128), F32),
               ("ind2", (128, 128), BF16),
               ("ind2e", (128, 128), BF16), ("bc2", (2, 128), BF16),
               ("onescol", (128, 1), BF16), ("onw", (128, 64), F32),
               ("onescol_f", (128, 2), F32), ("onesrow_f", (1, 128), F32),
               ("half2", (2, 128), F32)]

WEIGHT_SPECS = [("wq", (8, 128, 8, 128)), ("wk", (8, 128, 8, 128)),
                ("wv", (8, 128, 8, 128)), ("wo", (8, 128, 8, 128)),
                ("wb", (128, 8, 16)),
                ("cdq", (8, 128, 4, 128)), ("cdk", (8, 128, 4, 128)),
                ("cdv", (8, 128, 4, 128)),
                ("wgate", (32, 128, 8, 128)), ("wup", (32, 128, 8, 128)),
                ("wdown", (8, 128, 32, 128)),
                ("wqkv", (24, 128, 8, 128)),
                ("w1", (32, 128, 8, 128)), ("w2", (8, 128, 32, 128)),
                ("b1", (128, 32)), ("b2", (128, 8))]

QSPEC_L0 = [("wgate", "wgq", 32, 8, 0, 1e-5), ("wup", "wuq", 32, 8, 1, 1e-5),
            ("wdown", "wdq", 8, 32, 2, 1e-5)]
QSPEC_L1 = [("w1", "w1q", 32, 8, 3, 1e-8), ("w2", "w2q", 8, 32, 4, 1e-8)]


PHASES = 99
RELAX = None
A2_PARANOID = False
L8_STATS_PAR = False
L8_COPY_PAR = False
L8_LN_PAR = False
DUMP8 = "lni"
DBG_SCR = False
DUMP4 = False
SERIALIZE_DMA = True
APCHECK = False
GLOBAL_PARANOID = False
P4_PRE_PAR = False
P4_LN_PAR = False
LN_A_RELAX = False
LN_B_RELAX = False
LN_C_RELAX = False
LN_D_RELAX = False

class _Track:
    def __init__(self, P, cm):
        self.P = P
        self.cm = cm

    def __enter__(self):
        h = self.cm.__enter__()
        self.h = h
        self.P._alloc(h)
        return h

    def __exit__(self, *e):
        self.P._free(self.h)
        return self.cm.__exit__(*e)


def build_program():
    from contextlib import ExitStack
    nc = bass.Bass("TRN2", target_bir_lowering=False, debug=False)
    P = Prog(nc)

    def SBT(*args):
        return _Track(P, nc.sbuf_tensor(*args))

    din = {}
    din["x_in"] = nc.dram_tensor("x_in", [8, 128, T], F32, kind="ExternalInput").ap()
    for nm, shp in WEIGHT_SPECS:
        din[nm] = nc.dram_tensor(nm, list(shp), F32, kind="ExternalInput").ap()
    for nm, shp, dt in CONST_SPECS + MASK_SPECS:
        din[nm] = nc.dram_tensor(nm, list(shp), dt, kind="ExternalInput").ap()
    x_out = nc.dram_tensor("x_out", [8, 128, T], F32, kind="ExternalOutput").ap()
    s_out = nc.dram_tensor("s_out", [8, 128, T], F32,
                           kind="ExternalOutput").ap() if DBG_SCR else None
    dbg_raw = nc.dram_tensor("dbg_raw", [128, T + 8], BF16,
                             kind="ExternalOutput").ap() if DBG_SCR else None
    dbg_sil = nc.dram_tensor("dbg_sil", [128, T], BF16,
                             kind="ExternalOutput").ap() if DBG_SCR else None
    dbg_ps = nc.dram_tensor("dbg_ps", [128, 1024], F32,
                            kind="ExternalOutput").ap() if DBG_SCR else None
    if P.apc is not None:
        for _ap in din.values():
            P.apc.mark_pre(_ap.tensor)
    if GLOBAL_PARANOID:
        P.paranoid = True

    scr = {}
    for nm, shape, dt in [("qh", (8, 128, T), BF16), ("kh", (8, 128, T), BF16),
                          ("kbn", (8, 128, T), BF16), ("vb", (8, 128, T), BF16),
                          ("xn_d", (8, 128, T), BF16),
                          ("lnflat", (4096,), F32), ("bflat", (16, T), BF16),
                          ("nbflat", (16, T), BF16),
                          ("wgq", (32, 128, 8, 128), BF16), ("wuq", (32, 128, 8, 128), BF16),
                          ("wdq", (8, 128, 32, 128), BF16),
                          ("w1q", (32, 128, 8, 128), BF16), ("w2q", (8, 128, 32, 128), BF16),
                          ("wqB", (8, 128, 8, 128), BF16), ("wkB", (8, 128, 8, 128), BF16),
                          ("wvB", (8, 128, 8, 128), BF16), ("woB", (8, 128, 8, 128), BF16),
                          ("wbB", (128, 8, 16), BF16), ("cdqB", (8, 128, 4, 128), BF16),
                          ("cdkB", (8, 128, 4, 128), BF16), ("cdvB", (8, 128, 4, 128), BF16),
                          ("qkvB", (24, 128, 8, 128), BF16)]:
        scr[nm] = nc.dram_tensor("scr_" + nm, list(shape), dt,
                                 kind="ExternalOutput" if DBG_SCR else "Internal").ap()

    stack = ExitStack()
    E = stack.enter_context

    x_sb = E(SBT("x_sb", [128, NCH * T], F32))
    s_sb = E(SBT("s_sb", [128, NCH * T], BF16))
    consts = {}
    for nm, shape, dt in CONST_SPECS:
        consts[nm] = E(SBT("c_" + nm, list(shape), dt))
    cpool = E(SBT("cpool", [128, len(CONST_VALUES)], F32))
    for i, v in enumerate(CONST_VALUES):
        nc.const_aps.register(F32, float(v), cpool[:, i:i + 1])

    def capv(v):
        i = CONST_VALUES.index(v)
        return cpool[:, i:i + 1]

    _junkv = E(SBT("hazjunk", [128, 1024], BF16))
    P.haz_junk['vector'] = _junkv
    P.haz_junk['scalar'] = _junkv
    if P.apc is not None:
        P.apc.mark_pre(_junkv)

    S32 = E(SBT("S32", [128, 512], F32))
    Sbf = E(SBT("Sbf", [128, 512], BF16))
    swcol = E(SBT("swcol", [128, 48], F32))
    stat_acc = E(SBT("stat_acc", [128, 80], F32))
    cs16 = E(SBT("cs16", [16, 2], F32))
    t22 = E(SBT("t22", [2, 2], F32))
    lnrow_box = [None]
    lnrows32 = E(SBT("lnrows32", [32, 128], F32))

    PS = [E(nc.psum_tensor("ps%d" % i, [128, 512], F32)) for i in range(6)]
    PSB = [E(nc.psum_tensor("psb%d" % i, [128, 1024], BF16)) for i in range(2)]

    wbuf = [E(SBT("wbuf%d" % i, [128, 1024], BF16)) for i in range(4)]

    def _auto_r(fn, r):
        b = getattr(fn, 'kw', {}).get('bias', None)
        if isinstance(b, float) and b in CONST_VALUES:
            return list(r) + [(cpool, CONST_VALUES.index(b))]
        return r

    def TOP(fn, r=(), w=()):
        return P.op('tensor', fn, r, w)

    def VOP(fn, r=(), w=()):
        return P.op('vector', fn, _auto_r(fn, r), w)

    def SOP(fn, r=(), w=()):
        return P.op('scalar', fn, _auto_r(fn, r), w)

    def GOP(fn, r=(), w=()):
        return P.op('gpsimd', fn, _auto_r(fn, r), w)

    def DMA(eng, fn, r=(), w=()):
        if getattr(fn, 'method', '') == 'dma_start' and \
                getattr(fn.args[0].tensor, 'name', '') == 'x_out':
            eng = 'gpsimd'
        return P.op(eng, fn, r, w, dma=True)

    for nm, shape, dt in CONST_SPECS:
        DMA('sync', call('dma_start', consts[nm][:, :], din[nm]), w=[consts[nm]])
    for i, v in enumerate(CONST_VALUES):
        GOP(call('memset', cpool[:, i:i + 1], float(v)), w=[(cpool, i)])
    for ic in range(NCH):
        DMA('sync', call('dma_start', x_sb[:, ic * T:(ic + 1) * T], din["x_in"][ic]),
            w=[(x_sb, ic)])
    GOP(call('memset', S32[:, :], 0.0),
        w=[S32] + [(S32, g) for g in range(8)])
    GOP(call('memset', Sbf[:, :], 0.0), w=[(Sbf, g) for g in range(8)])

    def emit_stats(qspecs, wstg):
        for (nm, qnm, noc, n_ic, si, eps) in qspecs:
            src = din[nm]
            srcv = src.rearrange("o p i c -> o p (i c)")
            fsz = n_ic * 128
            nload = (noc * fsz) // 2048
            for b in range(nload):
                stg = wstg[b % len(wstg)]
                if fsz <= 2048:
                    opl = 2048 // fsz
                    srcs = [(stg[:, j * fsz:(j + 1) * fsz], srcv[b * opl + j])
                            for j in range(opl)]
                else:
                    hpb = fsz // 2048
                    srcs = [(stg[:, :], srcv[b // hpb][:, (b % hpb) * 2048:
                                                       (b % hpb) * 2048 + 2048])]
                for j, (dstap, inap) in enumerate(srcs):
                    DMA('sync', call('dma_start', dstap, inap), w=[(stg, j)])
                VOP(call('reduce_sum', stat_acc[:, si * 16 + b:si * 16 + b + 1],
                         stg[:, :], axis=AX.X, apply_absolute_value=True),
                    r=[(stg, j) for j in range(2)], w=[(stat_acc, (si, b))])
            TOP(call('matmul', PS[5][0:16, 0:2],
                     stat_acc[:, si * 16:si * 16 + 16], consts["onescol_f"][:, 0:2],
                     start=True, stop=True),
                r=[(stat_acc, (si, b)) for b in range(nload)] + [consts["onescol_f"]],
                w=[PS[5]])
            SOP(call('copy', cs16[0:16, 0:2], PS[5][0:16, 0:2]),
                r=[PS[5]], w=[cs16])
            TOP(call('matmul', PS[5][0:2, 4:6], cs16[0:16, 0:2],
                     consts["onescol_f"][0:16, 0:2], start=True, stop=True),
                r=[cs16, consts["onescol_f"]], w=[PS[5]])
            SOP(call('copy', t22[0:2, 0:2], PS[5][0:2, 4:6]),
                r=[PS[5]], w=[t22])
            TOP(call('matmul', PS[5][:, 8:10], consts["half2"][0:2, :],
                     t22[0:2, 0:2], start=True, stop=True),
                r=[t22, consts["half2"]], w=[PS[5]])
            SOP(call('activation', swcol[:, 8 * si:8 * si + 2], PS[5][:, 8:10],
                     AF.Relu, bias=-eps, scale=1.0 / (4096.0 * 1024.0)),
                r=[PS[5], (cpool, CONST_VALUES.index(-eps))], w=[(swcol, ('m0', si))])
            SOP(call('activation', swcol[:, 8 * si:8 * si + 2],
                     swcol[:, 8 * si:8 * si + 2], AF.Identity,
                     bias=eps, scale=1.0),
                r=[(swcol, ('m0', si)), (cpool, CONST_VALUES.index(eps))],
                w=[(swcol, ('m', si))])
            VOP(call('reciprocal', swcol[:, 8 * si + 2:8 * si + 4],
                     swcol[:, 8 * si:8 * si + 2]),
                r=[(swcol, ('m', si))], w=[(swcol, ('s', si))])
            if si < 3:
                SOP(call('mul', swcol[:, 8 * si + 4:8 * si + 6],
                         swcol[:, 8 * si:8 * si + 2], 0.5),
                    r=[(swcol, ('m', si))], w=[(swcol, ('halfm', si))])

    def emit_quant(qspecs, wstg, wsg):
        for (nm, qnm, noc, n_ic, si, eps) in qspecs:
            src, dst = din[nm], scr[qnm]
            srcv = src.rearrange("o p i c -> o p (i c)")
            dstv = dst.rearrange("o p i c -> o p (i c)")
            fsz = n_ic * 128
            nload = (noc * fsz) // 2048
            for b in range(nload):
                stg = wstg[b % len(wstg)]
                sgn = wsg[(2 * b) % len(wsg)]
                sgn2 = wsg[(2 * b + 1) % len(wsg)]
                if fsz <= 2048:
                    opl = 2048 // fsz
                    pieces = [(stg[:, j * fsz:(j + 1) * fsz], srcv[b * opl + j],
                               dstv[b * opl + j], sgn[:, j * fsz:(j + 1) * fsz])
                              for j in range(opl)]
                else:
                    hpb = fsz // 2048
                    sli = slice((b % hpb) * 2048, (b % hpb) * 2048 + 2048)
                    pieces = [(stg[:, :], srcv[b // hpb][:, sli],
                               dstv[b // hpb][:, sli], sgn[:, :])]
                for j, (dstap, inap, _, _) in enumerate(pieces):
                    DMA('sync', call('dma_start', dstap, inap), w=[(stg, j)])
                SOP(call('activation', sgn[:, :], stg[:, :], AF.Sign, bias=-0.5,
                         scale=swcol[:, 8 * si + 2:8 * si + 3]),
                    r=[(stg, j) for j in range(2)] + [(swcol, ('s', si))], w=[sgn])
                SOP(call('activation', sgn2[:, :], stg[:, :], AF.Sign, bias=0.5,
                         scale=swcol[:, 8 * si + 2:8 * si + 3]),
                    r=[(stg, j) for j in range(2)] + [(swcol, ('s', si))], w=[sgn2])
                VOP(call('tensor_add', sgn[:, :], sgn[:, :], sgn2[:, :]),
                    r=[sgn, sgn2], w=[sgn])
                for j, (_, _, outap, sgnap) in enumerate(pieces):
                    DMA('sync', call('dma_start', outap, sgnap),
                        r=[sgn], w=[(scr[qnm], b)])


    def emit_castpass(wstg, wsg):
        specs = [("wq", "wqB", 8 * 8), ("wk", "wkB", 8 * 8), ("wv", "wvB", 8 * 8),
                 ("wo", "woB", 8 * 8), ("wqkv", "qkvB", 24 * 8),
                 ("cdq", "cdqB", 8 * 4), ("cdk", "cdkB", 8 * 4), ("cdv", "cdvB", 8 * 4),
                 ("wb", "wbB", 1)]
        for idx, (nm, bnm, ntile) in enumerate(specs):
            if nm == "wb":
                srcv = din[nm].rearrange("p i c -> p (i c)")
                dstv = scr[bnm].rearrange("p i c -> p (i c)")
                blocks = [(srcv, dstv, 128)]
            else:
                srcv = din[nm].rearrange("o p i c -> o p (i c)") if nm not in ("cdq", "cdk", "cdv") else din[nm].rearrange("o p k c -> o p (k c)")
                dstv = scr[bnm].rearrange("o p i c -> o p (i c)") if nm not in ("cdq", "cdk", "cdv") else scr[bnm].rearrange("o p k c -> o p (k c)")
                noc = srcv.shape[0]
                fsz = srcv.shape[2]
                blocks = [(srcv[o], dstv[o], fsz) for o in range(noc)]
            for bi, (sap, dap, fsz) in enumerate(blocks):
                par = bi % 2
                stg = wstg[par]
                sg = wsg[par]
                DMA('sync', call('dma_start', stg[:, 0:fsz], sap), w=[(stg, 'cp')])
                eng = VOP if bi % 2 == 0 else SOP
                if bi % 2 == 0:
                    VOP(call('tensor_copy', sg[:, 0:fsz], stg[:, 0:fsz]),
                        r=[(stg, 'cp')], w=[(sg, 'cp')])
                else:
                    SOP(call('copy', sg[:, 0:fsz], stg[:, 0:fsz]),
                        r=[(stg, 'cp')], w=[(sg, 'cp')])
                DMA('sync', call('dma_start', dap, sg[:, 0:fsz]),
                    r=[(sg, 'cp')], w=[(scr[bnm], bi)])

    def emit_ln(xbf, x2stg, xn_dst, pF, xn_dram=None):
        lnrow = lnrow_box[0]
        base_par = P.paranoid
        def sec(f):
            P.paranoid = base_par and not f
        nblk = T // 128
        sec(LN_A_RELAX)
        for b in range(nblk):
            x2 = x2stg[b % len(x2stg)]
            for ic in range(NCH):
                VOP(call('tensor_mul', x2[:, ic * 128:(ic + 1) * 128],
                         xbf[:, ic * T + b * 128: ic * T + b * 128 + 128],
                         xbf[:, ic * T + b * 128: ic * T + b * 128 + 128]),
                    r=[(xbf, ic)], w=[(x2, ic)])
            for ic in range(NCH):
                TOP(call('matmul', PS[0][:, b:b + 1],
                         xbf[:, ic * T + b * 128: ic * T + b * 128 + 128],
                         consts["onescol"][:, :], start=(ic == 0), stop=False),
                    r=[(xbf, ic), consts["onescol"]], w=[PS[0]])
                TOP(call('matmul', PS[0][:, 16 + b:16 + b + 1],
                         x2[:, ic * 128:(ic + 1) * 128],
                         consts["onescol"][:, :], start=False, stop=(ic == NCH - 1)),
                    r=[(x2, ic), consts["onescol"]], w=[PS[0]])
        sec(LN_B_RELAX)
        VOP(call('tensor_scalar', pF[:, 0:32], PS[0][:, 0:32], 1.0 / D, None, ALU.mult),
            r=[PS[0]], w=[(pF, 'a')])
        VOP(call('tensor_tensor', pF[:, 32:48], pF[:, 0:16],
                 pF[:, 0:16], ALU.mult),
            r=[(pF, 'a')], w=[(pF, 'v')])
        VOP(call('tensor_tensor', pF[:, 32:48], pF[:, 16:32],
                 pF[:, 32:48], ALU.subtract),
            r=[(pF, 'a'), (pF, 'v')], w=[(pF, 'v')])
        if DUMP4 and PHASES == 4 and xn_dram is not None:
            DMA('sync', call('dma_start', x_out[3][:, 0:16], pF[:, 32:48]),
                r=[(pF, 'v')], w=[(x_out, 'f3a')])
        SOP(call('activation', pF[:, 48:64], pF[:, 32:48], AF.Ln, bias=1e-5, scale=1.0),
            r=[(pF, 'v')], w=[(pF, 'r')])
        SOP(call('activation', pF[:, 48:64], pF[:, 48:64], AF.Exp, bias=0.0,
                 scale=capv(-0.5)),
            r=[(pF, 'r'), (cpool, CONST_VALUES.index(-0.5))], w=[(pF, 'r')])
        if DUMP4 and PHASES == 4 and xn_dram is not None:
            DMA('sync', call('dma_start', x_out[3][:, 16:32], pF[:, 48:64]),
                r=[(pF, 'r')], w=[(x_out, 'f3b')])
        VOP(call('tensor_copy', pF[:, 64:80], pF[:, 0:16]),
            r=[(pF, 'a')], w=[(pF, 'p')])
        VOP(call('tensor_copy', pF[:, 80:96], pF[:, 48:64]),
            r=[(pF, 'r')], w=[(pF, 'p')])
        if DUMP4 and PHASES == 4 and xn_dram is not None:
            DMA('sync', call('dma_start', x_out[3][:, 32:64], pF[:, 64:96]),
                r=[(pF, 'p')], w=[(x_out, 'f3c')])
        sec(LN_C_RELAX)
        TOP(call('transpose', PS[1][0:32, 0:128], pF[:, 64:96], consts["identf"][:, :]),
            r=[(pF, 'p'), consts["identf"]], w=[PS[1]])
        VOP(call('tensor_copy', lnrows32[:, :], PS[1][0:32, 0:128]),
            r=[PS[1]], w=[lnrows32])
        DMA('sync', call('dma_start', scr['lnflat'].rearrange("(j t) -> j t", t=128),
                         lnrows32[:, :]),
            r=[lnrows32], w=[scr['lnflat']])
        sec(LN_D_RELAX)
        for tt in range(4):
            DMA('sync', call('dma_start', lnrow[0:1, 0:512],
                             scr['lnflat'][tt * 512:(tt + 1) * 512].rearrange("(o f) -> o f", o=1)),
                r=[scr['lnflat']], w=[(lnrow, 'm')])
            DMA('sync', call('dma_start', lnrow[0:1, 512:1024],
                             scr['lnflat'][2048 + tt * 512:2048 + (tt + 1) * 512].rearrange("(o f) -> o f", o=1)),
                r=[scr['lnflat']], w=[(lnrow, 'r')])
            for b4 in range(4):
                TOP(call('matmul', PS[2][:, b4 * 128:(b4 + 1) * 128],
                         consts["onesrow_f"][0:1, :],
                         lnrow[0:1, b4 * 128:(b4 + 1) * 128],
                         start=True, stop=True),
                    r=[(lnrow, 'm'), consts["onesrow_f"]], w=[PS[2]])
                TOP(call('matmul', PS[3][:, b4 * 128:(b4 + 1) * 128],
                         consts["onesrow_f"][0:1, :],
                         lnrow[0:1, 512 + b4 * 128:512 + (b4 + 1) * 128],
                         start=True, stop=True),
                    r=[(lnrow, 'r'), consts["onesrow_f"]], w=[PS[3]])
            for ic in range(NCH):
                VOP(call('tensor_tensor', pF[:, 32:544],
                         x_sb[:, ic * T + tt * SP: ic * T + tt * SP + SP],
                         PS[2][:, 0:512], ALU.subtract),
                    r=[(x_sb, ic), PS[2]],
                    w=[(pF, 'sub'), (pF, 'v'), (pF, 'r'), (pF, 'p')])
                VOP(call('tensor_tensor',
                         xn_dst[:, ic * T + tt * SP: ic * T + tt * SP + SP],
                         pF[:, 32:544], PS[3][:, 0:512], ALU.mult),
                    r=[(pF, 'sub'), PS[3]], w=[(xn_dst, ic)])
                if xn_dram is not None:
                    DMA('sync', call('dma_start',
                                     xn_dram[ic][:, tt * SP:(tt + 1) * SP],
                                     xn_dst[:, ic * T + tt * SP: ic * T + tt * SP + SP]),
                        r=[(xn_dst, ic)], w=[(scr['xn_d'], (ic, tt))])

    def emit_a1(xn, bb):
        cdbuf = bb['cdbuf']
        wbb = bb['wbb']
        DMA('sync', call('dma_start', wbb[:, :],
                         scr['wbB'].rearrange("p i c -> p (i c)")),
            r=[(scr['wbB'], 0)], w=[wbb])
        bpre, bbf, nbbf = bb['bpre'], bb['bbf'], bb['nbbf']
        for tt in range(4):
            for ic in range(NCH):
                TOP(call('matmul', PS[4][0:16, 0:512], wbb[:, ic * 16:(ic + 1) * 16],
                         xn[:, ic * T + tt * SP: ic * T + tt * SP + SP],
                         start=(ic == 0), stop=(ic == NCH - 1)),
                    r=[(xn, ic), wbb], w=[PS[4]])
            SOP(call('activation', bpre[0:16, 0:512], PS[4][0:16, 0:512], AF.Exp,
                     bias=0.0, scale=capv(-1.0)[0:16, 0:1]),
                r=[PS[4], (cpool, CONST_VALUES.index(-1.0))], w=[bpre])
            SOP(call('activation', bpre[0:16, 0:512], bpre[0:16, 0:512], AF.Ln,
                     bias=capv(1.0)[0:16, 0:1], scale=1.0),
                r=[bpre, (cpool, CONST_VALUES.index(1.0))], w=[bpre])
            SOP(call('activation', bbf[0:16, tt * SP:(tt + 1) * SP], bpre[0:16, 0:512],
                     AF.Exp, bias=0.0, scale=capv(-1.0)[0:16, 0:1]),
                r=[bpre], w=[(bbf, tt)])
            VOP(call('tensor_scalar', nbbf[0:16, tt * SP:(tt + 1) * SP],
                     bbf[0:16, tt * SP:(tt + 1) * SP], -1.0, None, ALU.mult),
                r=[(bbf, tt)], w=[(nbbf, tt)])

        bbf2, nbbf2 = bb['bbf2'], bb['nbbf2']
        DMA('sync', call('dma_start', scr['bflat'], bbf[0:16, :]),
            r=[(bbf, tt) for tt in range(4)], w=[scr['bflat']])
        DMA('sync', call('dma_start', scr['nbflat'], nbbf[0:16, :]),
            r=[(nbbf, tt) for tt in range(4)], w=[scr['nbflat']])
        for tname, wnm, cdnm in [('q', 'wqB', 'cdqB'), ('k', 'wkB', 'cdkB'),
                                 ('v', 'wvB', 'cdvB')]:
            wsrc = scr[wnm].rearrange("o p i c -> o p (i c)")
            cdsrc = scr[cdnm].rearrange("o p k c -> o p (k c)")
            for oc in range(NCH):
                par = oc % 2
                wb_ = wbuf[par]
                raw = bb['raw'][par]
                sil = bb['sil'][par]
                cd = cdbuf[par]
                DMA('sync', call('dma_start', wb_[:, :], wsrc[oc]),
                    r=[(scr[wnm], oc)], w=[wb_])
                DMA('sync', call('dma_start', cd[:, :], cdsrc[oc]),
                    r=[(scr[cdnm], oc)], w=[cd])
                psp = PS[par]
                psc = PS[2 + par]
                for tt in range(4):
                    for ic in range(NCH):
                        TOP(call('matmul', psp[:, 0:512],
                                 wb_[:, ic * 128:(ic + 1) * 128],
                                 xn[:, ic * T + tt * SP: ic * T + tt * SP + SP],
                                 start=(ic == 0), stop=(ic == NCH - 1)),
                            r=[(xn, ic), wb_], w=[psp])
                    VOP(call('tensor_copy', raw[:, 3 + tt * SP: 3 + tt * SP + SP],
                             psp[:, 0:512]),
                        r=[psp], w=[(raw, tt)])
                for tt in range(4):
                    rdeps = [(raw, tt), cd]
                    if tt > 0:
                        rdeps.append((raw, tt - 1))
                    else:
                        rdeps.append((raw, 'halo'))
                    for k in range(4):
                        TOP(call('matmul', psc[:, 0:512], cd[:, k * 128:(k + 1) * 128],
                                 raw[:, k + tt * SP: k + tt * SP + SP],
                                 start=(k == 0), stop=(k == 3)),
                            r=rdeps, w=[psc])
                    SOP(call('activation', sil[:, tt * SP:(tt + 1) * SP], psc[:, 0:512],
                             AF.Silu, bias=0.0, scale=1.0),
                        r=[psc], w=[(sil, tt)])
                if DBG_SCR and tname == 'v' and oc == 3:
                    DMA('sync', call('dma_start', dbg_raw, raw[:, :]),
                        r=[(raw, tt) for tt in range(4)] + [(raw, 'halo')],
                        w=[(dbg_raw, 0)])
                    DMA('sync', call('dma_start', dbg_sil, sil[:, :]),
                        r=[(sil, tt) for tt in range(4)], w=[(dbg_sil, 0)])
                    P.op('vector', call('engine_nop'),
                         reads=[(dbg_raw, 0), (dbg_sil, 0)])
                if tname == 'v':
                    ob = bb['ob1'][par]
                    DMA('sync', call('dma_start', bbf2[0:2, :],
                                     scr['bflat'][2 * oc:2 * oc + 2, :]),
                        r=[scr['bflat']], w=[bbf2])
                    for tt in range(4):
                        TOP(call('matmul', PS[4][:, 0:512], consts["bc2"][0:2, :],
                                 bbf2[0:2, tt * SP:(tt + 1) * SP],
                                 start=True, stop=True),
                            r=[bbf2, consts["bc2"]], w=[PS[4]])
                        VOP(call('tensor_tensor', ob[:, tt * SP:(tt + 1) * SP],
                                 PS[4][:, 0:512], sil[:, tt * SP:(tt + 1) * SP],
                                 ALU.mult),
                            r=[PS[4], (sil, tt)], w=[(ob, tt)])
                    DMA('sync', call('dma_start', scr['vb'][oc], ob[:, :]),
                        r=[(ob, tt) for tt in range(4)], w=[(scr['vb'], oc)])
                else:
                    sq = bb['sq'][0]
                    lnb = bb['lnb'][0]
                    rr = bb['rr'][par]
                    ob = bb['ob1'][par]
                    lim = 1.25e5 if tname == 'q' else 1e6
                    for tt in range(4):
                        VOP(call('tensor_mul', sq[:, 0:512],
                                 sil[:, tt * SP:(tt + 1) * SP],
                                 sil[:, tt * SP:(tt + 1) * SP]),
                            r=[(sil, tt)], w=[sq])
                        TOP(call('matmul', PS[4][:, 0:512], consts["ind2"][:, :],
                                 sq[:, 0:512], start=True, stop=True),
                            r=[sq, consts["ind2"]], w=[PS[4]])
                        SOP(call('activation', lnb[:, 0:512], PS[4][:, 0:512], AF.Ln,
                                 bias=1e-30, scale=1.0),
                            r=[PS[4], (cpool, CONST_VALUES.index(1e-30))], w=[lnb])
                        if tname == 'q':
                            SOP(call('activation', rr[:, 0:512], lnb[:, 0:512], AF.Exp,
                                     bias=capv(LN8), scale=capv(-0.5)),
                                r=[lnb, (cpool, CONST_VALUES.index(LN8))], w=[rr])
                        else:
                            SOP(call('activation', rr[:, 0:512], lnb[:, 0:512], AF.Exp,
                                     bias=0.0, scale=capv(-0.5)),
                                r=[lnb], w=[rr])
                        VOP(call('scalar_tensor_tensor', ob[:, tt * SP:(tt + 1) * SP],
                                 rr[:, 0:512], lim, sil[:, tt * SP:(tt + 1) * SP],
                                 ALU.min, ALU.mult),
                            r=[rr, (sil, tt)], w=[(ob, tt)])
                    dnm = 'qh' if tname == 'q' else 'kh'
                    DMA('sync', call('dma_start', scr[dnm][oc], ob[:, :]),
                        r=[(ob, tt) for tt in range(4)], w=[(scr[dnm], oc)])
                    if tname == 'k':
                        ob2 = bb['ob2'][par]
                        DMA('sync', call('dma_start', nbbf2[0:2, :],
                                         scr['nbflat'][2 * oc:2 * oc + 2, :]),
                            r=[scr['nbflat']], w=[nbbf2])
                        for tt in range(4):
                            TOP(call('matmul', PS[4][:, 0:512], consts["bc2"][0:2, :],
                                     nbbf2[0:2, tt * SP:(tt + 1) * SP],
                                     start=True, stop=True),
                                r=[nbbf2, consts["bc2"]], w=[PS[4]])
                            VOP(call('tensor_tensor', ob2[:, tt * SP:(tt + 1) * SP],
                                     PS[4][:, 0:512], ob[:, tt * SP:(tt + 1) * SP],
                                     ALU.mult),
                                r=[PS[4], (ob, tt)], w=[(ob2, tt)])
                        DMA('sync', call('dma_start', scr['kbn'][oc], ob2[:, :]),
                            r=[(ob2, tt) for tt in range(4)], w=[(scr['kbn'], oc)])

    def emit_a2(bb):
        def sec(name):
            if RELAX is not None:
                P.paranoid = name not in RELAX
        spq, spk, spkbn, spvb = bb['spq'], bb['spk'], bb['spkbn'], bb['spvb']
        pow_sb = bb['pow']
        Msb, Dtsb = bb['Msb'], bb['Dtsb']
        Xf, Xb, vbtk = bb['Xf'], bb['Xb'], bb['vbtk']
        otmn, o2s, ssd, rno, ofm = bb['otmn'], bb['o2s'], bb['ssd'], bb['rno'], bb['ofm']

        def ldspan(sp, par):
            sec('ldspan')
            for nm, buf in [('qh', spq), ('kh', spk), ('kbn', spkbn), ('vb', spvb)]:
                bufp = buf[par]
                for oc in range(NCH):
                    DMA('sync', call('dma_start', bufp[:, oc * SP:(oc + 1) * SP],
                                     scr[nm][oc][:, sp * SP:(sp + 1) * SP]),
                        r=[(scr[nm], oc)], w=[bufp])

        ldspan(0, 0)
        for sp in range(NSP):
            par = sp % 2
            if sp + 1 < NSP:
                ldspan(sp + 1, 1 - par)
            q_, k_, kbn_, vb_ = spq[par], spk[par], spkbn[par], spvb[par]
            for cb in range(SP // C):
                for w in range(2):
                    wp = w
                    gs = [4 * w + i for i in range(4)]
                    Ms, Dts, Xfw, Xbw, vk = Msb[0], Dtsb[0], Xf[wp], Xb[wp], vbtk[0]

                    def sl(buf, g, h):
                        return buf[64 * h:64 * h + 64,
                                   g * SP + cb * C: g * SP + cb * C + C]

                    sec('M')
                    for gi, g in enumerate(gs):
                        bank = PS[0] if gi < 2 else PS[1]
                        base = (gi % 2) * 256
                        for h in range(2):
                            TOP(call('matmul',
                                     bank[64 * h:64 * h + 64,
                                          base + 64 * h: base + 64 * h + 64],
                                     sl(kbn_, g, h), sl(k_, g, h),
                                     start=True, stop=True),
                                r=[kbn_, k_], w=[bank])
                            TOP(call('matmul',
                                     bank[64 * h:64 * h + 64,
                                          base + 128 + 64 * h: base + 128 + 64 * h + 64],
                                     sl(k_, g, h), sl(kbn_, g, h),
                                     start=True, stop=True),
                                r=[kbn_, k_], w=[bank])
                    for bi, bank in enumerate((PS[0], PS[1])):
                        VOP(call('scalar_tensor_tensor', Ms[:, bi * 512:(bi + 1) * 512],
                                 bank[:, 0:512], 1.0, consts["mmask2"][:, :],
                                 ALU.mult, ALU.mult),
                            r=[bank, consts["mmask2"]], w=[(Ms, bi)])
                    sec('Dt')
                    for gi, g in enumerate(gs):
                        for h in range(2):
                            TOP(call('matmul',
                                     PS[2][64 * h:64 * h + 64,
                                           gi * 128 + 64 * h: gi * 128 + 64 * h + 64],
                                     sl(k_, g, h), sl(q_, g, h), start=True, stop=True),
                                r=[k_, q_], w=[PS[2]])
                    VOP(call('scalar_tensor_tensor', Dts[:, 0:512], PS[2][:, 0:512],
                             1.0, consts["tiumask4"][:, :], ALU.mult, ALU.mult),
                        r=[PS[2], consts["tiumask4"]], w=[Dts])
                    sec('sq')
                    for lev in range(1, NLEV + 1):
                        for gi in range(4):
                            bank = PS[0] if gi < 2 else PS[1]
                            base = (gi % 2) * 256
                            if lev == 1:
                                srcR = Ms[:, gi * 256:gi * 256 + 128]
                                srcC = Ms[:, gi * 256 + 128:gi * 256 + 256]
                                rkey = (Ms, gi // 2)
                            else:
                                pv = pow_sb[lev - 2]
                                srcR = pv[:, gi * 256:gi * 256 + 128]
                                srcC = pv[:, gi * 256 + 128:gi * 256 + 256]
                                rkey = (pv, gi // 2)
                            if lev < NLEV:
                                TOP(call('matmul', bank[:, base:base + 128], srcC, srcR,
                                         start=True, stop=True),
                                    r=[rkey], w=[bank])
                            TOP(call('matmul', bank[:, base + 128:base + 256],
                                     srcR, srcC, start=True, stop=True),
                                r=[rkey], w=[bank])
                        pv = pow_sb[lev - 1]
                        for bi, bank in enumerate((PS[0], PS[1])):
                            VOP(call('tensor_copy', pv[:, bi * 512:(bi + 1) * 512],
                                     bank[:, 0:512]),
                                r=[bank], w=[(pv, bi)])
                    sec('trans')
                    for gi, g in enumerate(gs):
                        for h in range(2):
                            idh = consts["ident"][64 * h:64 * h + 64,
                                                  64 * h:64 * h + 64]
                            TOP(call('transpose',
                                     PSB[0][64 * h:64 * h + 64, gi * 64:gi * 64 + 64],
                                     sl(vb_, g, h), idh),
                                r=[vb_, consts["ident"]], w=[PSB[0]])
                            TOP(call('transpose',
                                     PSB[0][64 * h:64 * h + 64,
                                            256 + gi * 64: 256 + gi * 64 + 64],
                                     sl(k_, g, h), idh),
                                r=[k_, consts["ident"]], w=[PSB[0]])
                    VOP(call('tensor_copy', vk[:, 0:512], PSB[0][:, 0:512]),
                        r=[PSB[0]], w=[vk])
                    sec('R')
                    for gi, g in enumerate(gs):
                        for h in range(2):
                            TOP(call('matmul',
                                     PS[3][64 * h:64 * h + 64, gi * 64:gi * 64 + 64],
                                     sl(kbn_, g, h),
                                     Sbf[64 * h:64 * h + 64, g * 64:g * 64 + 64],
                                     start=True, stop=True),
                                r=[kbn_, (Sbf, g)], w=[PS[3]])
                    VOP(call('tensor_tensor', Xfw[:, 0:256], PS[3][:, 0:256],
                             vk[:, 0:256], ALU.add),
                        r=[PS[3], vk], w=[Xfw])
                    SOP(call('copy', Xbw[:, 0:256], Xfw[:, 0:256]),
                        r=[Xfw], w=[Xbw])
                    sec('app')
                    for lev in list(range(NLEV, 0, -1)) + [0]:
                        for gi in range(4):
                            if lev == 0:
                                lhs = Ms[:, gi * 256 + 128:gi * 256 + 256]
                                rkey = (Ms, gi // 2)
                            else:
                                pv = pow_sb[lev - 1]
                                lhs = pv[:, gi * 256 + 128:gi * 256 + 256]
                                rkey = (pv, gi // 2)
                            TOP(call('matmul', PS[3][:, 256 + gi * 64: 256 + gi * 64 + 64],
                                     lhs, Xbw[:, gi * 64:gi * 64 + 64],
                                     start=True, stop=True),
                                r=[rkey, Xbw], w=[PS[3]])
                        VOP(call('tensor_tensor', Xfw[:, 0:256], Xfw[:, 0:256],
                                 PS[3][:, 256:512], ALU.add),
                            r=[PS[3], Xfw], w=[Xfw])
                        SOP(call('copy', Xbw[:, 0:256], Xfw[:, 0:256]),
                            r=[Xfw], w=[Xbw])
                    sec('O')
                    for gi, g in enumerate(gs):
                        for h in range(2):
                            TOP(call('matmul',
                                     PS[4][64 * h:64 * h + 64, gi * 64:gi * 64 + 64],
                                     sl(q_, g, h),
                                     Sbf[64 * h:64 * h + 64, g * 64:g * 64 + 64],
                                     start=True, stop=False),
                                r=[q_, (Sbf, g)], w=[PS[4]])
                        TOP(call('matmul', PS[4][:, gi * 64:gi * 64 + 64],
                                 Dts[:, gi * 128:gi * 128 + 128],
                                 Xbw[:, gi * 64:gi * 64 + 64], start=False, stop=True),
                            r=[Dts, Xbw], w=[PS[4]])
                    sec('onorm')
                    for gi in range(4):
                        SOP(call('activation', o2s[:, 0:64],
                                 PS[4][:, gi * 64:gi * 64 + 64], AF.Square,
                                 bias=0.0, scale=1.0,
                                 accum_out=ssd[:, 4 * w + gi:4 * w + gi + 1]),
                            r=[PS[4]], w=[(ssd, 4 * w + gi), o2s])
                    SOP(call('activation', rno[:, 4 * w:4 * w + 4],
                             ssd[:, 4 * w:4 * w + 4], AF.Ln, bias=1e-6,
                             scale=1.0 / 64.0),
                        r=[(ssd, 4 * w + gi) for gi in range(4)] +
                          [(cpool, CONST_VALUES.index(1e-6))],
                        w=[(rno, w)])
                    SOP(call('activation', rno[:, 4 * w:4 * w + 4],
                             rno[:, 4 * w:4 * w + 4], AF.Exp, bias=0.0,
                             scale=capv(-0.5)),
                        r=[(rno, w)], w=[(rno, w)])
                    for gi in range(4):
                        VOP(call('scalar_tensor_tensor', otmn[:, gi * 64:gi * 64 + 64],
                                 PS[4][:, gi * 64:gi * 64 + 64],
                                 rno[:, 4 * w + gi:4 * w + gi + 1],
                                 consts["onw"][:, :], ALU.mult, ALU.mult),
                            r=[PS[4], (rno, w), consts["onw"]], w=[(otmn, gi)])
                    sec('otrans')
                    for gi, g in enumerate(gs):
                        for h in range(2):
                            idh = consts["ident"][64 * h:64 * h + 64,
                                                  64 * h:64 * h + 64]
                            TOP(call('transpose',
                                     PSB[1][64 * h:64 * h + 64, gi * 64:gi * 64 + 64],
                                     otmn[64 * h:64 * h + 64, gi * 64:gi * 64 + 64],
                                     idh),
                                r=[(otmn, gi), consts["ident"]], w=[PSB[1]])
                        VOP(call('tensor_copy',
                                 ofm[:, g * SP + cb * C: g * SP + cb * C + C],
                                 PSB[1][:, gi * 64:gi * 64 + 64]),
                            r=[PSB[1]], w=[(ofm, g)])
                    sec('supd')
                    for gi, g in enumerate(gs):
                        for h in range(2):
                            TOP(call('matmul',
                                     PS[5][64 * h:64 * h + 64, gi * 64:gi * 64 + 64],
                                     vk[64 * h:64 + 64 * h, 256 + gi * 64:256 + gi * 64 + 64],
                                     Xbw[64 * h:64 * h + 64, gi * 64:gi * 64 + 64],
                                     start=True, stop=True),
                                r=[vk, Xbw], w=[PS[5]])
                    VOP(call('tensor_tensor', S32[:, 256 * w:256 * w + 256],
                             S32[:, 256 * w:256 * w + 256], PS[5][:, 0:256], ALU.add),
                        r=[PS[5]] + [(S32, g) for g in gs],
                        w=[(S32, g) for g in gs])
                    SOP(call('copy', Sbf[:, 256 * w:256 * w + 256],
                             S32[:, 256 * w:256 * w + 256]),
                        r=[(S32, g) for g in gs], w=[(Sbf, g) for g in gs])
            sec('wo')
            wosrc = scr['woB'].rearrange("o p i c -> o p (i c)")
            for oc in range(NCH):
                wb_ = wbuf[2 + oc % 2]
                DMA('sync', call('dma_start', wb_[:, :], wosrc[oc]),
                    r=[(scr['woB'], oc)], w=[wb_])
                for ic in range(NCH):
                    TOP(call('matmul', PS[2][:, 0:512],
                             wb_[:, ic * 128:(ic + 1) * 128],
                             ofm[:, ic * SP:(ic + 1) * SP], start=(ic == 0),
                             stop=(ic == NCH - 1)),
                        r=[wb_, (ofm, ic)], w=[PS[2]])
                VOP(call('tensor_copy',
                         s_sb[:, oc * T + sp * SP: oc * T + sp * SP + SP],
                         PS[2][:, 0:512]),
                    r=[PS[2]], w=[(s_sb, (oc, sp))])

    def emit_l0b(bb):
        wbufD = bb['wbufD']
        xnt = bb['xnt']
        xq, hraw = bb['xq'], bb['hraw']
        mp, mn, mx4 = bb['mpos'], bb['mneg'], bb['mx4']
        mrows, mrow1 = bb['mrows'], bb['mrow1']
        sxb, sxib = bb['sxb'], bb['sxib']
        tq, gs_, tu = bb['tq'], bb['gs'], bb['tu']
        gq = tq
        wgv = scr['wgq'].rearrange("o p i c -> o p (i c)")
        wuv = scr['wuq'].rearrange("o p i c -> o p (i c)")
        wdv = scr['wdq'].rearrange("o p i c -> o p (i c)")

        def quant_scales():
            VOP(call('tensor_scalar', mn[:, 0:512], mn[:, 0:512], -1.0, None, ALU.mult),
                r=[mn], w=[mn])
            VOP(call('tensor_max', mp[:, 0:512], mp[:, 0:512], mn[:, 0:512]),
                r=[mp, mn], w=[mp])
            for b4 in range(4):
                TOP(call('transpose', PS[3][:, b4 * 128:(b4 + 1) * 128],
                         mp[:, b4 * 128:(b4 + 1) * 128], consts["identf"][:, :]),
                    r=[mp, consts["identf"]], w=[PS[3]])
            for b4 in range(4):
                VOP(call('reduce_max', mx4[:, b4:b4 + 1],
                         PS[3][:, b4 * 128:(b4 + 1) * 128], axis=AX.X),
                    r=[PS[3]], w=[(mx4, 'raw')])
            SOP(call('activation', mx4[:, 4:8], mx4[:, 0:4], AF.Relu,
                     bias=-1e-5, scale=1.0),
                r=[(mx4, 'raw'), (cpool, CONST_VALUES.index(-1e-5))], w=[(mx4, 'c0')])
            SOP(call('activation', mx4[:, 4:8], mx4[:, 4:8], AF.Identity,
                     bias=1e-5, scale=1.0),
                r=[(mx4, 'c0'), (cpool, CONST_VALUES.index(1e-5))], w=[(mx4, 'mc')])
            SOP(call('mul', mx4[:, 8:12], mx4[:, 4:8], 1.0 / 127.0),
                r=[(mx4, 'mc')], w=[(mx4, 'inv')])
            VOP(call('reciprocal', mx4[:, 12:16], mx4[:, 4:8]),
                r=[(mx4, 'mc')], w=[(mx4, 'rc')])
            SOP(call('mul', mx4[:, 12:16], mx4[:, 12:16], 127.0),
                r=[(mx4, 'rc')], w=[(mx4, 'sx')])
            TOP(call('transpose', PS[3][0:8, 0:128], mx4[:, 8:16], consts["identf"][:, :]),
                r=[(mx4, 'inv'), (mx4, 'sx'), consts["identf"]], w=[PS[3]])
            VOP(call('tensor_copy', mrows[:, :], PS[3][0:8, 0:128]),
                r=[PS[3]], w=[mrows])
            DMA('sync', call('dma_start',
                             scr['lnflat'][0:1024].rearrange("(j t) -> j t", t=128),
                             mrows[:, :]),
                r=[mrows], w=[scr['lnflat']])
            DMA('sync', call('dma_start', mrow1[0:1, 0:1024],
                             scr['lnflat'][0:1024].rearrange("(o f) -> o f", o=1)),
                r=[scr['lnflat']], w=[mrow1])
            for b4 in range(4):
                TOP(call('matmul', PS[3][:, b4 * 128:(b4 + 1) * 128],
                         consts["onesrow_f"][0:1, :],
                         mrow1[0:1, 512 + b4 * 128: 512 + (b4 + 1) * 128],
                         start=True, stop=True),
                    r=[mrow1, consts["onesrow_f"]], w=[PS[3]])
            VOP(call('tensor_copy', sxb[:, 0:512], PS[3][:, 0:512]),
                r=[PS[3]], w=[sxb])
            for b4 in range(4):
                TOP(call('matmul', PS[3][:, b4 * 128:(b4 + 1) * 128],
                         consts["onesrow_f"][0:1, :],
                         mrow1[0:1, b4 * 128:(b4 + 1) * 128], start=True, stop=True),
                    r=[mrow1, consts["onesrow_f"]], w=[PS[3]])
            VOP(call('tensor_copy', sxib[:, 0:512], PS[3][:, 0:512]),
                r=[PS[3]], w=[sxib])

        for tt in range(4):
            xt = xnt[0]
            for ic in range(NCH):
                DMA('sync', call('dma_start', xt[:, ic * SP:(ic + 1) * SP],
                                 scr['xn_d'][ic][:, tt * SP:(tt + 1) * SP]),
                    r=[(scr['xn_d'], (ic, tt))], w=[(xt, ic)])
            VOP(call('tensor_max', mp[:, 0:512], xt[:, 0:512], xt[:, 512:1024]),
                r=[(xt, 0), (xt, 1)], w=[mp])
            VOP(call('tensor_tensor', mn[:, 0:512], xt[:, 0:512], xt[:, 512:1024],
                     ALU.min),
                r=[(xt, 0), (xt, 1)], w=[mn])
            for ic in range(2, NCH):
                VOP(call('tensor_max', mp[:, 0:512], mp[:, 0:512],
                         xt[:, ic * SP:(ic + 1) * SP]),
                    r=[mp, (xt, ic)], w=[mp])
                VOP(call('tensor_tensor', mn[:, 0:512], mn[:, 0:512],
                         xt[:, ic * SP:(ic + 1) * SP], ALU.min),
                    r=[mn, (xt, ic)], w=[mn])
            quant_scales()
            for ic in range(NCH):
                VOP(call('tensor_tensor', tq[:, 0:512], xt[:, ic * SP:(ic + 1) * SP],
                         sxb[:, 0:512], ALU.mult),
                    r=[(xt, ic), sxb], w=[tq])
                VOP(call('tensor_scalar', xq[:, ic * SP:(ic + 1) * SP], tq[:, 0:512],
                         CRND, CRND, ALU.add, ALU.subtract),
                    r=[tq], w=[(xq, ic)])
            for oc in range(32):
                par = oc % 2
                wg_, wu_ = wbuf[par], wbuf[2 + par]
                DMA('sync', call('dma_start', wg_[:, :], wgv[oc]), w=[wg_])
                DMA('sync', call('dma_start', wu_[:, :], wuv[oc]), w=[wu_])
                psg, psu = PS[par], PS[2 + par]
                for ic in range(NCH):
                    TOP(call('matmul', psg[:, 0:512], wg_[:, ic * 128:(ic + 1) * 128],
                             xq[:, ic * SP:(ic + 1) * SP], start=(ic == 0),
                             stop=(ic == NCH - 1)),
                        r=[wg_, (xq, ic)], w=[psg])
                    TOP(call('matmul', psu[:, 0:512], wu_[:, ic * 128:(ic + 1) * 128],
                             xq[:, ic * SP:(ic + 1) * SP], start=(ic == 0),
                             stop=(ic == NCH - 1)),
                        r=[wu_, (xq, ic)], w=[psu])
                VOP(call('tensor_tensor', tq[:, 0:512], psg[:, 0:512], sxib[:, 0:512],
                         ALU.mult),
                    r=[psg, sxib], w=[tq])
                SOP(call('activation', gs_[:, 0:512], tq[:, 0:512], AF.Silu,
                         bias=0.0, scale=swcol[:, 4:5]),
                    r=[tq, (swcol, ('halfm', 0))], w=[gs_])
                SOP(call('activation', tu[:, 0:512], psu[:, 0:512], AF.Identity,
                         bias=0.0, scale=swcol[:, 12:13]),
                    r=[psu, (swcol, ('halfm', 1))], w=[tu])
                VOP(call('tensor_tensor', tu[:, 0:512], tu[:, 0:512], sxib[:, 0:512],
                         ALU.mult),
                    r=[tu, sxib], w=[tu])
                VOP(call('tensor_tensor', hraw[:, oc * 512:(oc + 1) * 512],
                         gs_[:, 0:512], tu[:, 0:512], ALU.mult),
                    r=[gs_, tu], w=[(hraw, oc)])
            VOP(call('tensor_max', mp[:, 0:512], hraw[:, 0:512], hraw[:, 512:1024]),
                r=[(hraw, 0), (hraw, 1)], w=[mp])
            VOP(call('tensor_tensor', mn[:, 0:512], hraw[:, 0:512], hraw[:, 512:1024],
                     ALU.min),
                r=[(hraw, 0), (hraw, 1)], w=[mn])
            for oc in range(2, 32):
                VOP(call('tensor_max', mp[:, 0:512], mp[:, 0:512],
                         hraw[:, oc * 512:(oc + 1) * 512]),
                    r=[mp, (hraw, oc)], w=[mp])
                VOP(call('tensor_tensor', mn[:, 0:512], mn[:, 0:512],
                         hraw[:, oc * 512:(oc + 1) * 512], ALU.min),
                    r=[mn, (hraw, oc)], w=[mn])
            quant_scales()
            for oc in range(32):
                VOP(call('tensor_tensor', gq[:, 0:512],
                         hraw[:, oc * 512:(oc + 1) * 512], sxb[:, 0:512], ALU.mult),
                    r=[(hraw, oc), sxb], w=[gq])
                VOP(call('tensor_scalar', hraw[:, oc * 512:(oc + 1) * 512], gq[:, 0:512],
                         CRND, CRND, ALU.add, ALU.subtract),
                    r=[gq], w=[(hraw, oc)])
            for oc2 in range(NCH):
                par = oc2 % 2
                wd_ = wbufD[par]
                DMA('sync', call('dma_start', wd_[:, :], wdv[oc2]), w=[wd_])
                psd = PS[4 + par]
                for ic in range(32):
                    TOP(call('matmul', psd[:, 0:512], wd_[:, ic * 128:(ic + 1) * 128],
                             hraw[:, ic * 512:(ic + 1) * 512], start=(ic == 0),
                             stop=(ic == 31)),
                        r=[wd_, (hraw, ic)], w=[psd])
                VOP(call('tensor_tensor', tq[:, 0:512], psd[:, 0:512], sxib[:, 0:512],
                         ALU.mult),
                    r=[psd, sxib], w=[tq])
                SOP(call('activation', gs_[:, 0:512], tq[:, 0:512], AF.Silu,
                         bias=0.0, scale=swcol[:, 20:21]),
                    r=[tq, (swcol, ('halfm', 2))], w=[gs_])
                VOP(call('tensor_tensor', tu[:, 0:512],
                         s_sb[:, oc2 * T + tt * SP: oc2 * T + tt * SP + SP],
                         gs_[:, 0:512], ALU.mult),
                    r=[(s_sb, (oc2, tt)), gs_], w=[tu])
                VOP(call('tensor_tensor',
                         x_sb[:, oc2 * T + tt * SP: oc2 * T + tt * SP + SP],
                         x_sb[:, oc2 * T + tt * SP: oc2 * T + tt * SP + SP],
                         tu[:, 0:512], ALU.add),
                    r=[tu, (x_sb, oc2)], w=[(x_sb, oc2)])

    def emit_l1a(xn1, bb):
        qoc, koc, voc, qk, cc = bb['qoc'], bb['koc'], bb['voc'], bb['qk'], bb['cc']
        embc, pmbc = bb['embc'], bb['pmbc']
        mch, smch, lnd = bb['mch'], bb['smch'], bb['lnd']
        emv, pmv = bb['emv'], bb['pmv']
        onesT, iota = bb['onesT'], bb['iota']
        wsrc = scr['qkvB'].rearrange("o p i c -> o p (i c)")
        GOP(call('iota', iota[:, :], [[1, T]], base=0, channel_multiplier=0, allow_small_or_imprecise_dtypes=True), w=[iota])
        GOP(call('memset', onesT[:, :], 1.0), w=[onesT])
        for g in range(NCH):
            for dst, ocw, key in ((qoc, g, 'q'), (koc, 8 + g, 'k'), (voc, 16 + g, 'v')):
                wb_ = wbuf[{'q': 0, 'k': 1, 'v': 2}[key]]
                DMA('sync', call('dma_start', wb_[:, :], wsrc[ocw]),
                    r=[(scr['qkvB'], ocw)], w=[wb_])
                for tt in range(4):
                    ps = PS[tt % 2]
                    for ic in range(NCH):
                        TOP(call('matmul', ps[:, 0:512],
                                 wb_[:, ic * 128:(ic + 1) * 128],
                                 xn1[:, ic * T + tt * SP: ic * T + tt * SP + SP],
                                 start=(ic == 0), stop=(ic == NCH - 1)),
                            r=[(xn1, ic), wb_], w=[ps])
                    VOP(call('tensor_copy', dst[:, tt * SP:(tt + 1) * SP], ps[:, 0:512]),
                        r=[ps], w=[(dst, tt)])
            VOP(call('tensor_mul', qk[:, :], qoc[:, :], koc[:, :]),
                r=[(qoc, tt) for tt in range(4)] + [(koc, tt) for tt in range(4)],
                w=[(qk, tt) for tt in range(4)])
            for tt in range(4):
                TOP(call('matmul', PS[2][:, 0:512], consts["ind2e"][:, :],
                         qk[:, tt * SP:(tt + 1) * SP], start=True, stop=True),
                    r=[(qk, tt), consts["ind2e"]], w=[PS[2]])
                VOP(call('tensor_scalar', mch[:, 0:512], PS[2][:, 0:512], 0.0, None,
                         ALU.max),
                    r=[PS[2]], w=[mch])
                VOP(call('tensor_scalar', smch[:, 0:512], PS[2][:, 0:512], 0.0, None,
                         ALU.min),
                    r=[PS[2]], w=[smch])
                SOP(call('activation', emv[:, 0:512], mch[:, 0:512], AF.Exp,
                         bias=0.0, scale=capv(-1.0)),
                    r=[mch], w=[emv])
                SOP(call('activation', pmv[:, 0:512], smch[:, 0:512], AF.Exp,
                         bias=0.0, scale=1.0),
                    r=[smch], w=[pmv])
                VOP(call('tensor_tensor', lnd[:, 0:512],
                         iota[:, tt * SP:(tt + 1) * SP], emv[:, 0:512], ALU.mult),
                    r=[emv, iota], w=[lnd])
                VOP(call('tensor_tensor', lnd[:, 0:512], lnd[:, 0:512], pmv[:, 0:512],
                         ALU.add),
                    r=[lnd, pmv], w=[lnd])
                SOP(call('activation', lnd[:, 0:512], lnd[:, 0:512], AF.Ln,
                         bias=0.0, scale=1.0),
                    r=[lnd], w=[lnd])
                VOP(call('tensor_tensor', mch[:, 0:512], mch[:, 0:512], lnd[:, 0:512],
                         ALU.add),
                    r=[mch, lnd], w=[mch])
                SOP(call('activation', embc[:, tt * SP:(tt + 1) * SP], mch[:, 0:512],
                         AF.Exp, bias=0.0, scale=capv(-1.0)),
                    r=[mch], w=[(embc, tt)])
                VOP(call('tensor_tensor', smch[:, 0:512], smch[:, 0:512], lnd[:, 0:512],
                         ALU.subtract),
                    r=[smch, lnd], w=[smch])
                SOP(call('activation', pmbc[:, tt * SP:(tt + 1) * SP], smch[:, 0:512],
                         AF.Exp, bias=0.0, scale=1.0),
                    r=[smch], w=[(pmbc, tt)])
            GOP(call('memset', cc[:, 0:1], 0.0), w=[(cc, 'z')])
            VOP(call('tensor_tensor_scan', cc[:, 1:T], onesT[:, 0:T - 1],
                     voc[:, 0:T - 1], 0.0, ALU.mult, ALU.add),
                r=[(voc, tt) for tt in range(4)] + [onesT],
                w=[(cc, tt) for tt in range(4)])
            for tt in range(4):
                VOP(call('tensor_tensor', qk[:, tt * SP:(tt + 1) * SP],
                         cc[:, tt * SP:(tt + 1) * SP],
                         embc[:, tt * SP:(tt + 1) * SP], ALU.mult),
                    r=[(cc, tt), (cc, 'z'), (embc, tt)], w=[(qk, tt)])
                VOP(call('tensor_tensor', cc[:, tt * SP:(tt + 1) * SP],
                         voc[:, tt * SP:(tt + 1) * SP],
                         pmbc[:, tt * SP:(tt + 1) * SP], ALU.mult),
                    r=[(voc, tt), (pmbc, tt), (cc, tt)], w=[(cc, tt)])
                VOP(call('tensor_tensor',
                         s_sb[:, g * T + tt * SP: g * T + tt * SP + SP],
                         qk[:, tt * SP:(tt + 1) * SP], cc[:, tt * SP:(tt + 1) * SP],
                         ALU.add),
                    r=[(qk, tt), (cc, tt)], w=[(s_sb, (g, tt))])

    def emit_l1b(xn1, bb):
        wbufD = bb['wbufD']
        hg, tq, tu, gs_ = bb['hg'], bb['tq'], bb['tu'], bb['gs']
        b1sb, b2sb = bb['b1sb'], bb['b2sb']
        w1v = scr['w1q'].rearrange("o p i c -> o p (i c)")
        w2v = scr['w2q'].rearrange("o p i c -> o p (i c)")
        DMA('sync', call('dma_start', b1sb[:, :], din['b1']), w=[b1sb])
        DMA('sync', call('dma_start', b2sb[:, :], din['b2']), w=[b2sb])
        for tt in range(4):
            for oc in range(32):
                par = oc % 2
                w1_ = wbuf[par]
                DMA('sync', call('dma_start', w1_[:, :], w1v[oc]), w=[w1_])
                ps = PS[par]
                for ic in range(NCH):
                    TOP(call('matmul', ps[:, 0:512], w1_[:, ic * 128:(ic + 1) * 128],
                             xn1[:, ic * T + tt * SP: ic * T + tt * SP + SP],
                             start=(ic == 0), stop=(ic == NCH - 1)),
                        r=[(xn1, ic), w1_], w=[ps])
                SOP(call('activation', hg[:, oc * 512:(oc + 1) * 512], ps[:, 0:512],
                         AF.Gelu, bias=b1sb[:, oc:oc + 1], scale=capv(0.5)),
                    r=[ps, b1sb, (cpool, CONST_VALUES.index(0.5))], w=[(hg, oc)])
            for oc2 in range(NCH):
                par = oc2 % 2
                w2_ = wbufD[par]
                DMA('sync', call('dma_start', w2_[:, :], w2v[oc2]), w=[w2_])
                psd = PS[4 + par]
                for ic in range(32):
                    TOP(call('matmul', psd[:, 0:512], w2_[:, ic * 128:(ic + 1) * 128],
                             hg[:, ic * 512:(ic + 1) * 512], start=(ic == 0),
                             stop=(ic == 31)),
                        r=[w2_, (hg, ic)], w=[psd])
                SOP(call('activation', gs_[:, 0:512], psd[:, 0:512], AF.Silu,
                         bias=b2sb[:, oc2:oc2 + 1], scale=capv(0.5)),
                    r=[psd, b2sb], w=[gs_])
                VOP(call('tensor_tensor', tu[:, 0:512],
                         s_sb[:, oc2 * T + tt * SP: oc2 * T + tt * SP + SP],
                         gs_[:, 0:512], ALU.mult),
                    r=[(s_sb, (oc2, tt)), gs_], w=[tu])
                VOP(call('tensor_tensor',
                         x_sb[:, oc2 * T + tt * SP: oc2 * T + tt * SP + SP],
                         x_sb[:, oc2 * T + tt * SP: oc2 * T + tt * SP + SP],
                         tu[:, 0:512], ALU.add),
                    r=[tu, (x_sb, oc2)], w=[(x_sb, oc2)])
                DMA('sync', call('dma_start', x_out[oc2][:, tt * SP:(tt + 1) * SP],
                                 x_sb[:, oc2 * T + tt * SP: oc2 * T + tt * SP + SP]),
                    r=[(x_sb, oc2)], w=[(x_out, (oc2, tt))])

    from contextlib import ExitStack as ES

    with ES() as ph:
        Eh = ph.enter_context
        wstg = [Eh(SBT("wstg%d" % i, [128, 2048], F32)) for i in range(2)]
        wsg = [Eh(SBT("wsg%d" % i, [128, 2048], BF16)) for i in range(2)]
        xbf = Eh(SBT("xbf", [128, NCH * T], BF16))
        x2s = [Eh(SBT("x2s%d" % i, [128, 1024], BF16)) for i in range(1)]
        pF = Eh(SBT("pF", [128, 544], F32))
        lnrow_box[0] = Eh(SBT("lnrow0", [1, 1024], F32))
        xn0 = xbf

        P.paranoid = P4_PRE_PAR or GLOBAL_PARANOID
        if PHASES >= 1:
            emit_stats(QSPEC_L0, wstg)
        if PHASES >= 2:
            emit_quant(QSPEC_L0, wstg, wsg)
        if PHASES >= 3:
            emit_castpass(wstg, wsg)
        if PHASES >= 4:
            for ic in range(NCH):
                if ic % 2 == 0:
                    VOP(call('tensor_copy', xbf[:, ic * T:(ic + 1) * T],
                             x_sb[:, ic * T:(ic + 1) * T]),
                        r=[(x_sb, ic)], w=[(xbf, ic)])
                else:
                    SOP(call('copy', xbf[:, ic * T:(ic + 1) * T],
                             x_sb[:, ic * T:(ic + 1) * T]),
                        r=[(x_sb, ic)], w=[(xbf, ic)])
        if PHASES >= 4:
            P.paranoid = P4_LN_PAR or GLOBAL_PARANOID
            emit_ln(xbf, x2s, xn0, pF, xn_dram=[scr['xn_d'][ic] for ic in range(NCH)])
        P.paranoid = GLOBAL_PARANOID
        P.barrier()
        if PHASES == 4 and DUMP4:
            dstg = wstg[0]
            DMA('sync', call('dma_start', x_out[0][:, 0:544], pF[:, :]),
                r=[(pF, k) for k in ('a', 'v', 'r', 'p', 'sub')],
                w=[(x_out, ('f', 0))])
            DMA('sync', call('dma_start', x_out[1][0:32, 0:128], lnrows32[:, :]),
                r=[lnrows32], w=[(x_out, ('f', 1))])
            VOP(call('tensor_copy', dstg[:, 0:32], PS[0][:, 0:32]),
                r=[PS[0]], w=[(dstg, 'd')])
            VOP(call('tensor_copy', dstg[:, 512:1024], xbf[:, 0:512]),
                r=[(xbf, 0)], w=[(dstg, 'd')])
            DMA('sync', call('dma_start', x_out[2], dstg[:, :]),
                r=[(dstg, 'd')], w=[(x_out, ('f', 2))])
            P.op('vector', call('engine_nop'),
                 reads=[(x_out, ('f', i)) for i in range(3)] +
                       [(x_out, k) for k in ('f3a', 'f3b', 'f3c')])

    with ES() as ph:
        Eh = ph.enter_context
        bb = {}
        bb['wbb'] = Eh(SBT("wbb", [128, 128], BF16))
        bb['bpre'] = Eh(SBT("bpre", [16, 512], F32))
        bb['bbf'] = Eh(SBT("bbf", [16, T], BF16))
        bb['nbbf'] = Eh(SBT("nbbf", [16, T], BF16))
        bb['bbf2'] = Eh(SBT("bbf2", [2, T], BF16))
        bb['nbbf2'] = Eh(SBT("nbbf2", [2, T], BF16))
        bb['cdbuf'] = [Eh(SBT("cdbuf%d" % i, [128, 512], BF16)) for i in range(2)]
        bb['raw'] = [Eh(SBT("raw%d" % i, [128, T + 8], BF16)) for i in range(2)]
        bb['sil'] = [Eh(SBT("sil%d" % i, [128, T], BF16)) for i in range(2)]
        bb['sq'] = [Eh(SBT("sqb%d" % i, [128, 512], BF16)) for i in range(1)]
        bb['lnb'] = [Eh(SBT("lnbb%d" % i, [128, 512], F32)) for i in range(1)]
        bb['rr'] = [Eh(SBT("rrb%d" % i, [128, 512], F32)) for i in range(2)]
        bb['ob1'] = [Eh(SBT("ob1%d" % i, [128, T], BF16)) for i in range(2)]
        bb['ob2'] = [Eh(SBT("ob2%d" % i, [128, T], BF16)) for i in range(2)]
        xn0b = Eh(SBT("xn0b", [128, NCH * T], BF16))
        if PHASES >= 5:
            for ic in range(NCH):
                DMA('sync', call('dma_start', xn0b[:, ic * T:(ic + 1) * T], scr['xn_d'][ic]),
                    r=[(scr['xn_d'], (ic, tt)) for tt in range(4)], w=[(xn0b, ic)])
            for i in range(2):
                GOP(call('memset', bb['raw'][i][:, 0:3], 0.0), w=[(bb['raw'][i], 'halo')])
        if PHASES >= 5:
            emit_a1(xn0b, bb)
        P.barrier()

    with ES() as ph:
        Eh = ph.enter_context
        bb = {}
        bb['spq'] = [Eh(SBT("spq%d" % i, [128, NCH * SP], BF16)) for i in range(2)]
        bb['spk'] = [Eh(SBT("spk%d" % i, [128, NCH * SP], BF16)) for i in range(2)]
        bb['spkbn'] = [Eh(SBT("spkbn%d" % i, [128, NCH * SP], BF16)) for i in range(2)]
        bb['spvb'] = [Eh(SBT("spvb%d" % i, [128, NCH * SP], BF16)) for i in range(2)]
        bb['pow'] = [Eh(SBT("pow%d" % i, [128, 1024], BF16)) for i in range(NLEV)]
        bb['Msb'] = [Eh(SBT("Msb%d" % i, [128, 1024], BF16)) for i in range(1)]
        bb['Dtsb'] = [Eh(SBT("Dtsb%d" % i, [128, 512], BF16)) for i in range(1)]
        bb['Xf'] = [Eh(SBT("Xf%d" % i, [128, 256], F32)) for i in range(2)]
        bb['Xb'] = [Eh(SBT("Xb%d" % i, [128, 256], BF16)) for i in range(2)]
        bb['vbtk'] = [Eh(SBT("vbtk%d" % i, [128, 512], BF16)) for i in range(1)]
        bb['otmn'] = Eh(SBT("otmn", [128, 256], BF16))
        bb['o2s'] = Eh(SBT("o2s", [128, 64], F32))
        bb['ssd'] = Eh(SBT("ssd", [128, 8], F32))
        bb['rno'] = Eh(SBT("rno", [128, 8], F32))
        bb['ofm'] = Eh(SBT("ofm", [128, NCH * SP], BF16))
        for nm, shape, dt in MASK_SPECS:
            consts[nm] = Eh(SBT("c_" + nm, list(shape), dt))
            DMA('sync', call('dma_start', consts[nm][:, :], din[nm]), w=[consts[nm]])
        if PHASES >= 6:
            if A2_PARANOID:
                P.paranoid = True
            emit_a2(bb)
            P.paranoid = False
        P.barrier()

    with ES() as ph:
        Eh = ph.enter_context
        bb = {}
        bb['xnt'] = [Eh(SBT("xnt0", [128, NCH * SP], BF16))]
        bb['xq'] = Eh(SBT("xq", [128, NCH * SP], BF16))
        bb['hraw'] = Eh(SBT("hraw", [128, 32 * 512], FP16))
        bb['wbufD'] = [Eh(SBT("wbufD%d" % i, [128, 4096], BF16)) for i in range(2)]
        bb['mpos'] = Eh(SBT("mpos", [128, 512], F32))
        bb['mneg'] = Eh(SBT("mneg", [128, 512], F32))
        bb['mx4'] = Eh(SBT("mx4", [128, 16], F32))
        bb['mrows'] = Eh(SBT("mrows", [8, 128], F32))
        bb['mrow1'] = Eh(SBT("mrow1", [1, 1024], F32))
        bb['sxb'] = Eh(SBT("sxb", [128, 512], F32))
        bb['sxib'] = Eh(SBT("sxib", [128, 512], F32))
        bb['tq'] = Eh(SBT("tqb", [128, 512], F32))
        bb['gs'] = Eh(SBT("gsb", [128, 512], F32))
        bb['tu'] = Eh(SBT("tub", [128, 512], F32))
        if PHASES >= 7:
            emit_l0b(bb)
        P.barrier()

    with ES() as ph:
        Eh = ph.enter_context
        xn1 = Eh(SBT("xn1", [128, NCH * T], BF16))
        with ES() as pl:
            El = pl.enter_context
            xbf1 = xn1
            x2s1 = [El(SBT("x2t%d" % i, [128, 1024], BF16)) for i in range(1)]
            pF1 = El(SBT("pF1", [128, 544], F32))
            lnrow_box[0] = El(SBT("lnrow1", [1, 1024], F32))
            wstg1 = [El(SBT("wstg1_%d" % i, [128, 2048], F32)) for i in range(1)]
            wsg1 = [El(SBT("wsg1_%d" % i, [128, 2048], BF16)) for i in range(2)]
            if PHASES >= 8:
                P.paranoid = L8_STATS_PAR
                emit_stats(QSPEC_L1, wstg1)
                emit_quant(QSPEC_L1, wstg1, wsg1)
            P.paranoid = L8_COPY_PAR
            for ic in range(NCH):
                if ic % 2 == 0:
                    VOP(call('tensor_copy', xbf1[:, ic * T:(ic + 1) * T],
                             x_sb[:, ic * T:(ic + 1) * T]),
                        r=[(x_sb, ic)], w=[(xbf1, ic)])
                else:
                    SOP(call('copy', xbf1[:, ic * T:(ic + 1) * T],
                             x_sb[:, ic * T:(ic + 1) * T]),
                        r=[(x_sb, ic)], w=[(xbf1, ic)])
            if PHASES >= 8:
                P.paranoid = L8_LN_PAR
                emit_ln(xbf1, x2s1, xn1, pF1)
            P.paranoid = False
            P.barrier()
            if PHASES == 8 and DUMP8 == 'lni':
                dstg = El(SBT("dbgstg", [128, T], F32))
                P.barrier()
                DMA('sync', call('dma_start', x_out[0][:, 0:544], pF1[:, :]),
                    r=[(pF1, k) for k in ('a', 'v', 'r', 'p', 'sub')],
                    w=[(x_out, ('f', 0))])
                DMA('sync', call('dma_start', x_out[1][0:32, 0:128], lnrows32[:, :]),
                    r=[lnrows32], w=[(x_out, ('f', 1))])
                DMA('sync', call('dma_start', x_out[2][0:1, 0:1024],
                                 lnrow_box[0][0:1, :]),
                    r=[(lnrow_box[0], 'm'), (lnrow_box[0], 'r')],
                    w=[(x_out, ('f', 2))])
                VOP(call('tensor_copy', dstg[:, 0:512], PS[2][:, 0:512]),
                    r=[PS[2]], w=[(dstg, 'd')])
                VOP(call('tensor_copy', dstg[:, 512:1024], PS[3][:, 0:512]),
                    r=[PS[3]], w=[(dstg, 'd')])
                VOP(call('tensor_copy', dstg[:, 1024:1056], PS[0][:, 0:32]),
                    r=[PS[0]], w=[(dstg, 'd')])
                DMA('sync', call('dma_start', x_out[3], dstg[:, :]),
                    r=[(dstg, 'd')], w=[(x_out, ('f', 3))])
                VOP(call('tensor_copy', dstg[:, :], xbf1[:, 0:T]),
                    r=[(xbf1, 0)], w=[(dstg, 'd')])
                DMA('sync', call('dma_start', x_out[4], dstg[:, :]),
                    r=[(dstg, 'd')], w=[(x_out, ('f', 4))])
                VOP(call('tensor_copy', dstg[:, :], xn1[:, 0:T]),
                    r=[(xn1, 0)], w=[(dstg, 'd')])
                DMA('sync', call('dma_start', x_out[5], dstg[:, :]),
                    r=[(dstg, 'd')], w=[(x_out, ('f', 5))])
                DMA('sync', call('dma_start', x_out[6], x_sb[:, 0:T]),
                    r=[(x_sb, 0)], w=[(x_out, ('f', 6))])
                VOP(call('tensor_copy', dstg[:, :], xn1[:, 7 * T:8 * T]),
                    r=[(xn1, 7)], w=[(dstg, 'd')])
                DMA('sync', call('dma_start', x_out[7], dstg[:, :]),
                    r=[(dstg, 'd')], w=[(x_out, ('f', 7))])
            elif PHASES == 8:
                dstg = El(SBT("dbgstg", [128, T], F32))
                dsrc = {'xn1': xn1, 'xbf1': xbf1}.get(DUMP8)
                for ic in range(NCH):
                    if dsrc is None:
                        DMA('sync', call('dma_start', x_out[ic],
                                         x_sb[:, ic * T:(ic + 1) * T]),
                            r=[(x_sb, ic)], w=[(x_out, ('f', ic))])
                        continue
                    VOP(call('tensor_copy', dstg[:, :], dsrc[:, ic * T:(ic + 1) * T]),
                        r=[(dsrc, ic)], w=[(dstg, 'd')])
                    DMA('sync', call('dma_start', x_out[ic], dstg[:, :]),
                        r=[(dstg, 'd')], w=[(x_out, ('f', ic))])

        with ES() as pa:
            Ea = pa.enter_context
            bb = {}
            bb['qoc'] = Ea(SBT("qoc", [128, T], BF16))
            bb['koc'] = Ea(SBT("koc", [128, T], BF16))
            bb['voc'] = Ea(SBT("voc", [128, T], BF16))
            bb['qk'] = Ea(SBT("qkb", [128, T], BF16))
            bb['cc'] = Ea(SBT("ccb", [128, T], F32))
            bb['embc'] = Ea(SBT("embc", [128, T], BF16))
            bb['pmbc'] = Ea(SBT("pmbc", [128, T], BF16))
            bb['mch'] = Ea(SBT("mch", [128, 512], F32))
            bb['smch'] = Ea(SBT("smch", [128, 512], F32))
            bb['lnd'] = Ea(SBT("lnd", [128, 512], F32))
            bb['emv'] = Ea(SBT("emv", [128, 512], F32))
            bb['pmv'] = Ea(SBT("pmv", [128, 512], F32))
            bb['onesT'] = Ea(SBT("onesT", [128, T], BF16))
            bb['iota'] = Ea(SBT("iotab", [128, T], F32))
            if PHASES >= 9:
                emit_l1a(xn1, bb)
            P.barrier()

        with ES() as pa:
            Ea = pa.enter_context
            bb = {}
            bb['hg'] = Ea(SBT("hgb", [128, 32 * 512], BF16))
            bb['tq'] = Ea(SBT("tqc", [128, 512], F32))
            bb['tu'] = Ea(SBT("tuc", [128, 512], F32))
            bb['gs'] = Ea(SBT("gsc", [128, 512], F32))
            bb['wbufD'] = [Ea(SBT("wbufE%d" % i, [128, 4096], BF16)) for i in range(2)]
            bb['b1sb'] = Ea(SBT("b1sb", [128, 32], F32))
            bb['b2sb'] = Ea(SBT("b2sb", [128, 8], F32))
            if PHASES >= 10:
                emit_l1b(xn1, bb)

    if PHASES >= 10:
        P.op('vector', call('engine_nop'),
             reads=[(x_out, (oc2, tt)) for oc2 in range(NCH) for tt in range(4)])
    else:
        if PHASES == 8 or (PHASES == 4 and DUMP4):
            pass
        elif PHASES == 9:
            dstg = stack.enter_context(SBT("dbgstg9", [128, T], F32))
            for ic in range(NCH):
                VOP(call('tensor_copy', dstg[:, :], s_sb[:, ic * T:(ic + 1) * T]),
                    r=[(s_sb, (ic, tt)) for tt in range(4)], w=[(dstg, 'd')])
                DMA('sync', call('dma_start', x_out[ic], dstg[:, :]),
                    r=[(dstg, 'd')], w=[(x_out, ('f', ic))])
        else:
            for ic in range(NCH):
                DMA('sync', call('dma_start', x_out[ic], x_sb[:, ic * T:(ic + 1) * T]),
                    r=[(x_sb, ic)], w=[(x_out, ('f', ic))])
        P.op('vector', call('engine_nop'),
             reads=[(x_out, ('f', ic)) for ic in range(NCH)])

    if DBG_SCR:
        dstg2 = stack.enter_context(SBT("dbgstg2", [128, T], F32))
        for ic in range(NCH):
            VOP(call('tensor_copy', dstg2[:, :], s_sb[:, ic * T:(ic + 1) * T]),
                r=[(s_sb, (ic, tt)) for tt in range(4)], w=[(dstg2, 'd')])
            DMA('sync', call('dma_start', s_out[ic], dstg2[:, :]),
                r=[(dstg2, 'd')], w=[(s_out, ('s', ic))])
        P.op('vector', call('engine_nop'),
             reads=[(s_out, ('s', ic)) for ic in range(NCH)])

    P.finalize()
    stack.close()
    return nc


_NC_CACHE = [None]


def kernel(**inputs):
    inputs = {k: np.asarray(v) for k, v in inputs.items()}
    H = _host_prep(inputs)
    x = inputs['L0_attn_x']
    if _NC_CACHE[0] is None:
        _NC_CACHE[0] = build_program()
    nc = _NC_CACHE[0]
    in_maps = []
    for b in range(8):
        m = {"x_in": np.ascontiguousarray(x[b].T).reshape(8, 128, T)}
        for nm, _ in WEIGHT_SPECS:
            m[nm] = H[nm]
        for nm, _, _ in CONST_SPECS + MASK_SPECS:
            m[nm] = H[nm]
        in_maps.append(m)
    res = run_bass_kernel_spmd(nc, in_maps, core_ids=list(range(8)))
    outs = []
    for r in res.results:
        o = np.asarray(r["x_out"]).reshape(1024, T)
        outs.append(np.ascontiguousarray(o.T))
    return np.stack(outs)
